# version 02
# baseline (speedup 1.0000x reference)
import os
import numpy as np
import concourse.bass as bass
import concourse.mybir as mybir
from concourse.bass_utils import run_bass_kernel_spmd

F32 = mybir.dt.float32
BF16 = mybir.dt.bfloat16
I32 = mybir.dt.int32
AF = mybir.ActivationFunctionType
ALU = mybir.AluOpType
AX = mybir.AxisListType
ET = mybir.EngineType

D = 2048
TOK = 2048
H = 16
HD = 128
M = 4
NCH = 16
IF = 8192
NMEGA = 2
EPS_LN = 1e-5
EPS_AT = 1e-6
SCALE = HD ** -0.5

STAGE = os.environ.get("MK_STAGE", "full")
NQR = int(os.environ.get("MK_NQ", "4"))
LNLVL = int(os.environ.get("MK_LN", "4"))
NOSPARS = os.environ.get("MK_NOSPARS", "0") == "1"
NHEADS = int(os.environ.get("MK_NH", "16"))
USE_EX = os.environ.get("MK_EX", "1") == "1"
NOMEM = os.environ.get("MK_NOMEM", "0") == "1"
NOLOC = os.environ.get("MK_NOLOC", "0") == "1"

OFF_A = 20448
OFF_B = 85984
DYN0 = 151520

ENGS = ("sync", "tensor", "vector", "scalar", "gpsimd")


def build():
    nc = bass.Bass("TRN2", target_bir_lowering=False, monotonic_sem_count=0)
    Q = {e: [] for e in ENGS}
    CNT = {}
    SEM = {}

    def sem(name):
        SEM[name] = nc.alloc_semaphore(name)
        CNT[name] = 0

    def em(eng, fn):
        Q[eng].append(fn)

    WMAX = {e: {} for e in ENGS}

    def wait(eng, name, val=None):
        v = CNT[name] if val is None else val
        if v <= 0 or WMAX[eng].get(name, 0) >= v:
            return
        WMAX[eng][name] = v
        em(eng, lambda e, h=SEM[name], v=v: e.wait_ge(h, v))

    def op(eng, makefn, incs=()):
        hs = []
        for s, k in incs:
            CNT[s] += k
            hs.append((SEM[s], k))

        def run(e, makefn=makefn, hs=hs):
            ins = makefn(e)
            for h, k in hs:
                ins = ins.then_inc(h, k)
        em(eng, run)

    bar_i = [0]

    def barrier():
        bar_i[0] += 1
        s = f"bar{bar_i[0]}"
        sem(s)
        for e in ENGS:
            op(e, lambda en, h=SEM[s]: en.sem_inc(h, 1))
        CNT[s] = 5
        for e in ENGS:
            wait(e, s, 5)

    dt_ = nc.dram_tensor
    x_in = dt_("x_in", [TOK, D], F32, kind="ExternalInput").ap()
    wq_d = dt_("wq", [D, D], F32, kind="ExternalInput").ap()
    wk_d = dt_("wk", [D, D], F32, kind="ExternalInput").ap()
    wv_d = dt_("wv", [D, D], F32, kind="ExternalInput").ap()
    wo_d = dt_("wo", [D, D], F32, kind="ExternalInput").ap()
    wg_d = dt_("wg_in", [D, IF], F32, kind="ExternalInput").ap()
    wu_d = dt_("wu_in", [D, IF], F32, kind="ExternalInput").ap()
    wd_d = dt_("wd_in", [IF, D], F32, kind="ExternalInput").ap()
    mem_d = dt_("mem_in", [M, H, HD, HD], F32, kind="ExternalInput").ap()
    mn_d = dt_("mn_in", [M, H, HD], F32, kind="ExternalInput").ap()
    gate_d = dt_("gate_in", [1, H], F32, kind="ExternalInput").ap()
    trim_d = dt_("trimask_in", [128, 128], BF16, kind="ExternalInput").ap()
    idb_d = dt_("identb_in", [128, 128], BF16, kind="ExternalInput").ap()
    onc_d = dt_("onescol_in", [128, 1], BF16, kind="ExternalInput").ap()
    onr_d = dt_("onesrow_in", [1, 128], BF16, kind="ExternalInput").ap()
    ust_d = dt_("usest_in", [128, 1], F32, kind="ExternalInput").ap()
    pid_d = dt_("pidx_in", [1, 2], I32, kind="ExternalInput").ap()
    flg_d = dt_("flagv_in", [1, 24], F32, kind="ExternalInput").ap()

    y = dt_("y", [TOK, D], F32, kind="ExternalOutput").ap()
    x1s = dt_("x1_scr", [TOK, D], F32, kind="Internal").ap()
    exd = dt_("exd", [2 * H, 128, 136], F32, kind="Internal",
              addr_space="Shared").ap()
    exf = dt_("exf", [2, 32], F32, kind="Internal", addr_space="Shared").ap()

    A_ = nc.alloc_sbuf_tensor
    trimask = A_("trimask", [128, 128], BF16).ap()
    identb = A_("identb", [128, 128], BF16).ap()
    onescol = A_("onescol", [128, 1], BF16).ap()
    onesrow = A_("onesrow", [1, 128], BF16).ap()
    usest = A_("usest", [128, 1], F32).ap()
    pidxt = A_("pidxt", [1, 2], I32).ap()
    flagt = A_("flagt", [1, 24], F32).ap()
    flagin = A_("flagin", [1, 8], F32).ap()
    gatet = A_("gatet", [1, H], F32).ap()
    sigg = A_("sigg", [1, H], F32).ap()
    omg = A_("omg", [1, H], F32).ap()
    etile = A_("etile", [1, 4], F32).ap()
    bneg = A_("bneg", [1, 4], F32).ap()
    em1 = A_("em1", [1, 4], F32).ap()
    mnT = A_("mnT", [128, 64], BF16).ap()
    contrib = A_("contrib", [128, 136], F32).ap()
    exin = A_("exin", [128, 136], F32).ap()
    srem = A_("srem", [128, 132], BF16).ap()
    stat1 = A_("stat1", [128, 1], F32).ap()
    stat2 = A_("stat2", [128, 1], F32).ap()
    stat3 = A_("stat3", [128, 1], F32).ap()
    qsumt = A_("qsumt", [128, 4], F32).ap()
    qsum_bf = A_("qsum_bf", [128, 1], BF16).ap()
    reltot = A_("reltot", [1, 4], F32).ap()
    relsc = A_("relsc", [1, 8], F32).ap()
    wfin = A_("wfin", [1, 8], F32).ap()
    wfin_bf = A_("wfin_bf", [1, 8], BF16).ap()
    wrep = A_("wrep", [128, 8], F32).ap()
    denA = A_("denA", [128, 16], F32).ap()
    denR = A_("denR", [128, 16], F32).ap()
    rdent = A_("rdent", [128, 16], F32).ap()

    _pr = A_("pr_probe", [128, 8], F32)
    _pa = nc.lookup_mloc(_pr).addr
    assert _pa + 32 <= OFF_A, f"persistent SBUF region overflows into BUF_A: {_pa}"

    def at(name, shape, dtype, off):
        assert off + int(np.prod(shape[1:])) * (2 if dtype == BF16 else 4) <= 229344, name
        return nc.alloc_sbuf_tensor_at(name, shape, dtype, offset=off).ap()

    BUF_A = at("BUF_A", [128, 16 * 2048], BF16, OFF_A)
    BUF_B = at("BUF_B", [128, 16 * 2048], BF16, OFF_B)

    wgq = [at(f"wgq{i}", [128, 16 * 512], BF16, DYN0 + 16384 * i) for i in range(2)]
    wuq = at("wuq", [128, 16 * 512], BF16, DYN0 + 32768)
    xt0 = at("xt0", [128, 2048], F32, DYN0 + 49152)
    tacc = at("tacc", [128, 2048], F32, DYN0 + 49152)
    xlnsc = at("xlnsc", [128, 2048], BF16, DYN0 + 57344)
    hgate = at("hgate", [128, 512], F32, DYN0 + 61440)
    hgateB = at("hgateB", [128, 512], F32, DYN0 + 73728)
    spa = [at(f"spa{i}", [128, 512], BF16, DYN0 + 63488 + 1024 * i) for i in range(4)]
    spc = [at(f"spc{i}", [128, 512], BF16, DYN0 + 67584 + 1024 * i) for i in range(6)]
    wdq1 = at("wdq1", [128, 16 * 2048], BF16, OFF_A)

    wT = [at(f"wT{i}", [128, 2048], BF16, DYN0 + 4096 * i) for i in range(6)]
    wstg = [at(f"wstg{i}", [128, 2048], BF16, DYN0 + 24576 + 4096 * i) for i in range(2)]
    sq_bf = at("sq_bf", [128, 2048], BF16, DYN0 + 32768)
    sk_bf = at("sk_bf", [128, 2048], BF16, DYN0 + 36864)
    v_bf = at("v_bf", [128, 2048], BF16, DYN0 + 40960)
    kpT = at("kpT", [128, 2048], BF16, DYN0 + 45056)
    vT = at("vT", [128, 2048], BF16, DYN0 + 49152)
    spv = at("spv", [128, 16 * 132], BF16, DYN0 + 53248)
    numb = at("numb", [128, 16 * 128], F32, DYN0 + 57600)
    memh = [at(f"memh{i}", [128, 512], BF16, DYN0 + 65792 + 1024 * i) for i in range(2)]
    p2t = [at(f"p2t{i}", [128, 512], F32, DYN0 + 67840 + 2048 * i) for i in range(2)]
    p2out = at("p2out", [128, 128], F32, DYN0 + 71936)
    scT = [at(f"scT{i}", [128, 128], BF16, DYN0 + 72448 + 256 * i) for i in range(2)]
    sfacall = at("sfacall", [128, 64], F32, DYN0 + 72960)
    x1t2 = at("x1t2", [128, 2048], F32, DYN0 + 32768)
    xln2sc = at("xln2sc", [128, 2048], BF16, DYN0 + 45056)

    woT = at("woT", [128, 16 * 2048], BF16, DYN0 + 0)
    wostg = at("wostg", [128, 2048], BF16, DYN0 + 65536)
    x2t = at("x2t", [128, 2048], F32, DYN0 + 69632)
    xln3sc = at("xln3sc", [128, 2048], BF16, DYN0 + 65536)

    wdh4 = at("wdh4", [128, 8 * 2048], BF16, DYN0 + 0)
    wuq4 = at("wuq4", [128, 16 * 512], BF16, DYN0 + 32768)
    hgate4 = at("hgate4", [128, 512], F32, DYN0 + 49152)
    hgate4B = at("hgate4B", [128, 512], F32, DYN0 + 69632)
    spa4 = [at(f"spa4_{i}", [128, 512], BF16, DYN0 + 51200 + 1024 * i) for i in range(4)]
    spc4 = [at(f"spc4_{i}", [128, 512], BF16, DYN0 + 55296 + 1024 * i) for i in range(6)]
    tacc4 = at("tacc4", [128, 2048], F32, DYN0 + 61440)
    wgq4 = [at(f"wgq4_{i}", [128, 16 * 512], BF16, OFF_B + 32768 + 16384 * i)
            for i in range(2)]

    PF = [nc.alloc_psum_tensor(f"PF{i}", [128, 512], F32).ap() for i in range(6)]
    PB1 = nc.alloc_psum_tensor("PB1", [128, 1024], BF16).ap()
    PB2 = nc.alloc_psum_tensor("PB2", [128, 1024], BF16).ap()

    for s in ("sx", "sw", "sacc", "spe", "sdv", "sac", "sexw", "spoll",
              "sexr", "sy"):
        sem(s)

    x_t = x_in.rearrange("(t p) d -> t p d", p=128)
    x1_t = x1s.rearrange("(t p) d -> t p d", p=128)
    y_t = y.rearrange("(t p) d -> t p d", p=128)

    regs = {}

    TP = {"idx": 0, "busy": [None, None]}
    ESEM = {"scalar": "sac", "vector": "sdv"}

    def tpx(src_ap, dst_ap, evac_eng="scalar", ncols=128, ident=None):
        b = TP["idx"] % 2
        TP["idx"] += 1
        bank = PB1 if b == 0 else PB2
        if TP["busy"][b] is not None:
            s, v = TP["busy"][b]
            wait("tensor", s, v)
        idn = identb if ident is None else ident
        op("tensor", lambda e, bank=bank, src_ap=src_ap, idn=idn, ncols=ncols:
           e.transpose(bank[0:src_ap.shape[1], 0:ncols], src_ap, idn), [("spe", 1)])
        m = CNT["spe"]
        wait(evac_eng, "spe", m)
        sname = ESEM[evac_eng]
        if evac_eng == "scalar":
            op(evac_eng, lambda e, dst_ap=dst_ap, bank=bank, src_ap=src_ap, ncols=ncols:
               e.copy(out=dst_ap, in_=bank[0:src_ap.shape[1], 0:ncols]), [(sname, 1)])
        else:
            op(evac_eng, lambda e, dst_ap=dst_ap, bank=bank, src_ap=src_ap, ncols=ncols:
               e.tensor_copy(out=dst_ap, in_=bank[0:src_ap.shape[1], 0:ncols]),
               [(sname, 1)])
        TP["busy"][b] = (sname, CNT[sname])
        return CNT[sname]

    for dst, src in ((trimask, trim_d[:, :]), (identb, idb_d[:, :]),
                     (onescol, onc_d[:, :]), (onesrow, onr_d[:, :]),
                     (usest, ust_d[:, :]), (pidxt, pid_d[:, :]),
                     (flagt, flg_d[:, :]), (gatet, gate_d[:, :])):
        op("sync", lambda e, dst=dst, src=src: e.dma_start(out=dst, in_=src),
           [("sx", 16)])
    mn64 = mn_d.rearrange("m h d -> (m h) d")
    mn4 = mn_d.rearrange("m h d -> m (h d)")
    op("sync", lambda e: e.dma_start(out=xt0[0:64, 0:128], in_=mn64[:, :]), [("sx", 16)])
    op("sync", lambda e: e.dma_start(out=xt0[64:68, 0:2048], in_=mn4[:, :]), [("sx", 16)])
    wait("scalar", "sx")
    op("scalar", lambda e: e.activation(out=sigg, in_=gatet, func=AF.Sigmoid), [("sac", 1)])
    wait("vector", "sac")
    op("vector", lambda e: e.tensor_scalar(out=omg, in0=sigg, scalar1=-1.0, scalar2=1.0,
                                           op0=ALU.mult, op1=ALU.add), [("sdv", 1)])
    wait("vector", "sx")
    op("vector", lambda e: e.tensor_copy(out=spa[0][0:64, 0:128], in_=xt0[0:64, 0:128]),
       [("sdv", 1)])
    wait("tensor", "sdv")
    tpx(spa[0][0:64, 0:128], mnT, "vector", ncols=64, ident=identb[0:64, 0:64])
    op("vector", lambda e: e.reduce_sum(out=stat1[0:4, 0:1], in_=xt0[64:68, 0:2048],
                                        axis=AX.X), [("sdv", 1)])
    wait("vector", "sdv")
    op("vector", lambda e: e.tensor_copy(out=spa[1][0:4, 0:1], in_=stat1[0:4, 0:1]),
       [("sdv", 1)])
    wait("tensor", "sdv")
    tpx(spa[1][0:4, 0:1], relsc[0:1, 4:8], "vector", ncols=4, ident=identb[0:4, 0:4])
    wait("vector", "sdv")
    op("vector", lambda e: e.tensor_single_scalar(out=etile, in_=relsc[0:1, 4:8],
                                                  scalar=1e-6, op=ALU.is_lt), [("sdv", 1)])
    wait("vector", "sdv")
    op("vector", lambda e: e.tensor_scalar_mul(out=bneg, in0=etile, scalar1=-1e30),
       [("sdv", 1)])
    op("vector", lambda e: e.tensor_scalar(out=em1, in0=etile, scalar1=-1.0, scalar2=1.0,
                                           op0=ALU.mult, op1=ALU.add), [("sdv", 1)])
    P0_DV = CNT["sdv"]
    P0_PE = CNT["spe"]
    wait("gpsimd", "sx")

    def _loadregs(e):
        regs["pvr"] = e.alloc_register("pvr")
        regs["fl"] = e.alloc_register("flpoll")
        e.reg_load(regs["pvr"], pidxt[0:1, 0:1])
    em("gpsimd", _loadregs)

    lnst = {"ln": [], "tp": []}

    def ln_tile(xsrc, dst_col_fn, sq_scr, ready=None):
        if LNLVL < 2:
            return CNT["sac"]
        if ready is not None:
            wait("vector", ready[0], ready[1])
        op("vector", lambda e, xsrc=xsrc: e.reduce_sum(out=stat1, in_=xsrc, axis=AX.X),
           [("sdv", 1)])
        wait("vector", "sdv")
        op("vector", lambda e: e.tensor_scalar_mul(out=stat1, in0=stat1, scalar1=-1.0 / D),
           [("sdv", 1)])
        wait("scalar", "sdv")
        op("scalar", lambda e, xsrc=xsrc, sq_scr=sq_scr: e.activation(
            out=sq_scr, in_=xsrc, func=AF.Square, bias=stat1[:, 0:1], scale=1.0,
            accum_out=stat2[:, 0:1]), [("sac", 1)])
        wait("vector", "sac")
        op("vector", lambda e: e.tensor_scalar(out=stat3, in0=stat2, scalar1=1.0 / D,
                                               scalar2=EPS_LN, op0=ALU.mult, op1=ALU.add),
           [("sdv", 1)])
        wait("scalar", "sdv")
        op("scalar", lambda e: e.sqrt(out=stat3, in_=stat3), [("sac", 1)])
        wait("vector", "sac")
        op("vector", lambda e: e.reciprocal(out=stat3, in_=stat3), [("sdv", 1)])
        if LNLVL < 3:
            lnst["ln"].append(CNT["sdv"])
            return CNT["sac"]
        wait("scalar", "sdv")
        op("scalar", lambda e, xsrc=xsrc, sq_scr=sq_scr: e.activation(
            out=sq_scr, in_=xsrc, func=AF.Identity, bias=stat1[:, 0:1], scale=1.0),
           [("sac", 1)])
        wait("vector", "sac")
        op("vector", lambda e, sq_scr=sq_scr: e.tensor_scalar_mul(
            out=sq_scr, in0=sq_scr, scalar1=stat3[:, 0:1]), [("sdv", 1)])
        lnst["ln"].append(CNT["sdv"])
        if LNLVL < 4:
            return CNT["sac"]
        for kc in range(16):
            wait("tensor", "sdv", lnst["ln"][-1])
            mk = tpx(sq_scr[:, kc * 128:(kc + 1) * 128], dst_col_fn(kc), "scalar")
            lnst["tp"].append(mk)
        return CNT["sac"]

    ffst = {"acc_tile": {}, "acc_busy": {}}

    def ffn_pass(phase):
        if phase == 1:
            hq, hq_base = BUF_B, 16384
            wgb, wub, wdb = wgq, wuq, wdq1
            spa_, spc_, hg_, tac_ = spa, spc, (hgate, hgateB), tacc
            out_tiles, nspan = x1_t, 4
        else:
            hq, hq_base = BUF_B, 0
            wgb, wub, wdb = wgq4, wuq4, wdh4
            spa_, spc_, hg_, tac_ = spa4, spc4, (hgate4, hgate4B), tacc4
            out_tiles, nspan = y_t, 2

        wg_r = wg_d.rearrange("(kc p) (g i) -> g p kc i", p=128, i=512)
        wu_r = wu_d.rearrange("(kc p) (g i) -> g p kc i", p=128, i=512)
        wd_r = wd_d.rearrange("(s g c m) e -> s c g m e", g=nspan, c=128, m=4)

        st = {"wg_used": [0, 0], "wu_used": 0, "wd_used": 0,
              "bk_g": [0, 0], "bk_u": [0, 0], "wd_evac": [0, 0], "spars": 0}

        for mg in range(NMEGA):
            if phase == 1:
                for i in range(8):
                    t = mg * 8 + i
                    if lnst["ln"]:
                        wait("sync", "sdv", lnst["ln"][-1])
                    else:
                        wait("sync", "sdv", P0_DV)
                    if mg > 0 and i == 0:
                        wait("sync", "sacc", CNT["sacc"])
                    op("sync", lambda e, t=t: e.dma_start(out=xt0, in_=x_t[t]),
                       [("sx", 16)])
                    wait("sync", "sx")
                    op("sync", lambda e, t=t: e.dma_start(out=x1_t[t], in_=xt0),
                       [("sacc", 16)])
                    ffst["acc_tile"][(1, t)] = CNT["sacc"]
                    ln_tile(xt0, lambda kc, i=i: BUF_B[:, kc * 1024 + i * 128:
                                                       kc * 1024 + (i + 1) * 128],
                            xlnsc, ready=("sx", CNT["sx"]))
                if i == 7:
                    st["bk_g"] = [max(st["bk_g"][0], CNT["sac"]),
                                  max(st["bk_g"][1], CNT["sac"])]
            for Qr in range(NQR):
                for g in range(4):
                    gq = Qr * 4 + g
                    wslot = gq % 2
                    wait("gpsimd", "spe", st["wg_used"][wslot])
                    op("gpsimd", lambda e, gq=gq, wslot=wslot, wgb=wgb: e.dma_start(
                        out=wgb[wslot][:, :].rearrange("p (kc i) -> p kc i", kc=16),
                        in_=wg_r[gq]), [("sw", 16)])
                    wg_ld = CNT["sw"]
                    wait("gpsimd", "spe", st["wu_used"])
                    op("gpsimd", lambda e, gq=gq, wub=wub: e.dma_start(
                        out=wub[:, :].rearrange("p (kc i) -> p kc i", kc=16),
                        in_=wu_r[gq]), [("sw", 16)])
                    wu_ld = CNT["sw"]
                    for p in range(2):
                        for m in range(4):
                            mp = m % 2
                            gbk = PF[4 + mp]
                            ubk = PF[2 + mp]
                            wait("tensor", "sw", wg_ld)
                            wait("tensor", "sac", st["bk_g"][mp])
                            for kc in range(16):
                                if phase == 1:
                                    rv = hq[:, kc * 1024 + p * 512:
                                            kc * 1024 + (p + 1) * 512]
                                else:
                                    rv = BUF_A[:, kc * 2048 + mg * 1024 + p * 512:
                                               kc * 2048 + mg * 1024 + (p + 1) * 512]
                                lv = wgb[wslot][:, kc * 512 + m: (kc + 1) * 512: 4]
                                op("tensor", lambda e, gbk=gbk, lv=lv, rv=rv, kc=kc:
                                   e.matmul(gbk[:, :], lv, rv, start=(kc == 0),
                                            stop=(kc == 15)),
                                   [("spe", 1)] if kc == 15 else ())
                            st["wg_used"][wslot] = CNT["spe"]
                            gmark = CNT["spe"]
                            wait("tensor", "sw", wu_ld)
                            wait("tensor", "sdv", st["bk_u"][mp])
                            for kc in range(16):
                                if phase == 1:
                                    rv = hq[:, kc * 1024 + p * 512:
                                            kc * 1024 + (p + 1) * 512]
                                else:
                                    rv = BUF_A[:, kc * 2048 + mg * 1024 + p * 512:
                                               kc * 2048 + mg * 1024 + (p + 1) * 512]
                                lv = wub[:, kc * 512 + m: (kc + 1) * 512: 4]
                                op("tensor", lambda e, ubk=ubk, lv=lv, rv=rv, kc=kc:
                                   e.matmul(ubk[:, :], lv, rv, start=(kc == 0),
                                            stop=(kc == 15)),
                                   [("spe", 1)] if kc == 15 else ())
                            st["wu_used"] = CNT["spe"]
                            umark = CNT["spe"]
                            wait("scalar", "spe", gmark)
                            wait("scalar", "sdv", st["bk_u"][mp])
                            hgb = hg_[mp]
                            op("scalar", lambda e, gbk=gbk, hgb=hgb: e.activation(
                                out=hgb, in_=gbk[:, :], func=AF.Silu), [("sac", 1)])
                            st["bk_g"][mp] = CNT["sac"]
                            hcol = hq_base + (g * 4 + m) * 1024 + p * 512
                            wait("vector", "sac", st["bk_g"][mp])
                            wait("vector", "spe", umark)
                            op("vector", lambda e, ubk=ubk, hcol=hcol, hgb=hgb, hq=hq:
                               e.tensor_mul(out=hq[:, hcol:hcol + 512], in0=hgb,
                                            in1=ubk[:, :]), [("sdv", 1)])
                            st["bk_u"][mp] = CNT["sdv"]
                        if NOSPARS:
                            continue
                        hcs = [hq_base + (g * 4 + mm) * 1024 + p * 512 for mm in range(4)]
                        wait("scalar", "sdv", CNT["sdv"])
                        for mm in range(4):
                            op("scalar", lambda e, hc=hcs[mm], mm=mm, hq=hq, spa_=spa_:
                               e.activation(out=spa_[mm], in_=hq[:, hc:hc + 512],
                                            func=AF.Abs), [("sac", 1)])
                        wait("vector", "sac")
                        prs = [(0, 1), (0, 2), (0, 3), (1, 2), (1, 3), (2, 3)]
                        for ci, (a, b) in enumerate(prs):
                            op("vector", lambda e, ci=ci, a=a, b=b, spa_=spa_, spc_=spc_:
                               e.tensor_tensor(out=spc_[ci], in0=spa_[a], in1=spa_[b],
                                               op=ALU.is_ge), [("sdv", 1)])
                        combos = [((0, 1, ALU.add), (2, ALU.add), ALU.is_ge, 1.5),
                                  ((3, 4, ALU.add), (0, ALU.subtract), ALU.is_ge, 0.5),
                                  ((1, 3, ALU.add), (5, ALU.subtract), ALU.is_le, 0.5),
                                  ((2, 4, ALU.add), (5, ALU.add), ALU.is_le, 1.5)]
                        for mm, ((ca, cb, o1), (cc, o2), kop, kth) in enumerate(combos):
                            op("vector", lambda e, mm=mm, ca=ca, cb=cb, o1=o1, spa_=spa_,
                               spc_=spc_: e.tensor_tensor(out=spa_[mm], in0=spc_[ca],
                                                          in1=spc_[cb], op=o1),
                               [("sdv", 1)])
                            op("vector", lambda e, mm=mm, cc=cc, o2=o2, spa_=spa_,
                               spc_=spc_: e.tensor_tensor(out=spa_[mm], in0=spa_[mm],
                                                          in1=spc_[cc], op=o2),
                               [("sdv", 1)])
                            op("vector", lambda e, mm=mm, kop=kop, kth=kth, spa_=spa_:
                               e.tensor_single_scalar(out=spa_[mm], in_=spa_[mm],
                                                      scalar=kth, op=kop), [("sdv", 1)])
                            op("vector", lambda e, mm=mm, hc=hcs[mm], hq=hq, spa_=spa_:
                               e.tensor_tensor(out=hq[:, hc:hc + 512],
                                               in0=hq[:, hc:hc + 512], in1=spa_[mm],
                                               op=ALU.mult), [("sdv", 1)])
                st["spars"] = CNT["sdv"]
                for half in range(4 // nspan):
                    wait("gpsimd", "spe", st["wd_used"])
                    sidx = Qr * (4 // nspan) + half
                    op("gpsimd", lambda e, sidx=sidx, wdb=wdb, nspan=nspan: e.dma_start(
                        out=wdb[:, :].rearrange("p (g m e) -> p g m e", g=nspan, m=4),
                        in_=wd_r[sidx]), [("sw", 16)])
                    wd_ld = CNT["sw"]
                    wait("tensor", "sw", wd_ld)
                    wait("tensor", "sdv", st["spars"])
                    for tt in range(8):
                        tglob = mg * 8 + tt
                        for es in range(4):
                            par = (tt * 4 + es) % 2
                            bk = PF[par]
                            wait("tensor", "sdv", st["wd_evac"][par])
                            glist = list(range(nspan))
                            nmm = nspan * 4
                            ci = 0
                            for g2 in glist:
                                gg = half * nspan + g2
                                for m in range(4):
                                    hcol = hq_base + (gg * 4 + m) * 1024 + tt * 128
                                    wcol = (g2 * 4 + m) * 2048 + es * 512
                                    ci += 1
                                    op("tensor", lambda e, bk=bk, hcol=hcol, wcol=wcol,
                                       s0=(ci == 1), s1=(ci == nmm), hq=hq, wdb=wdb:
                                       e.matmul(bk[:, :], hq[:, hcol:hcol + 128],
                                                wdb[:, wcol:wcol + 512], start=s0,
                                                stop=s1),
                                       [("spe", 1)] if ci == nmm else ())
                            st["wd_used"] = CNT["spe"]
                            wmark = CNT["spe"]
                            wait("vector", "spe", wmark)
                            if es == 0:
                                wait("vector", "sacc",
                                     ffst["acc_busy"].get(phase, 0))
                            op("vector", lambda e, bk=bk, es=es, tac_=tac_:
                               e.tensor_scalar_mul(
                                   out=tac_[:, es * 512:(es + 1) * 512],
                                   in0=bk[:, :], scalar1=0.5), [("sdv", 1)])
                            st["wd_evac"][par] = CNT["sdv"]
                        if True:
                            wait("gpsimd", "sdv", st["wd_evac"][(tt * 4 + 3) % 2])
                            wait("gpsimd", "sacc",
                                 ffst["acc_tile"].get((phase, tglob), 0))
                            op("gpsimd", lambda e, tglob=tglob, tac_=tac_,
                               out_tiles=out_tiles: e.dma_start(
                                   out=out_tiles[tglob], in_=tac_[:, :],
                                   accum_op=ALU.add), [("sacc", 16)])
                            ffst["acc_tile"][(phase, tglob)] = CNT["sacc"]
                            ffst["acc_busy"][phase] = CNT["sacc"]

    if STAGE == "p0":
        for t in range(16):
            wait("sync", "sdv", P0_DV)
            op("sync", lambda e, t=t: e.dma_start(out=xt0, in_=x_t[t]), [("sx", 16)])
            wait("sync", "sx")
            op("sync", lambda e, t=t: e.dma_start(out=y_t[t], in_=xt0), [("sy", 16)])
        wait("sync", "sy", CNT["sy"])
        wait("tensor", "spe", P0_PE)
        return nc, Q

    ffn_pass(1)
    barrier()

    if STAGE == "p1":
        wait("sync", "sacc", CNT["sacc"])
        for t in range(16):
            wait("sync", "sy", max(0, CNT["sy"] - 16))
            op("sync", lambda e, t=t: e.dma_start(out=xt0, in_=x1_t[t]), [("sx", 16)])
            wait("sync", "sx")
            op("sync", lambda e, t=t: e.dma_start(out=y_t[t], in_=xt0), [("sy", 16)])
        wait("sync", "sy", CNT["sy"])
        return nc, Q

    for t in ([] if os.environ.get("MK_LN2", "1") == "0" else range(16)):
        if lnst["ln"]:
            wait("sync", "sdv", lnst["ln"][-1])
        op("sync", lambda e, t=t: e.dma_start(out=x1t2, in_=x1_t[t]), [("sx", 16)])
        ln_tile(x1t2, lambda kc, t=t: BUF_A[:, kc * 2048 + t * 128:
                                           kc * 2048 + (t + 1) * 128],
                xln2sc, ready=("sx", CNT["sx"]))
    LN2_DONE = CNT["sac"]

    wprep = {}

    def prep_head_weights(hh):
        sl = hh % 2
        srcs = (wq_d, wk_d, wv_d)
        for wi in range(3):
            src = srcs[wi].rearrange("(h p) k -> h p k", p=128)[hh]
            stg = wstg[wi % 2]
            wait("gpsimd", "spe", wprep.get(("stg", wi % 2), 0))
            wait("gpsimd", "spe", wprep.get(("slot", sl), 0))
            op("gpsimd", lambda e, stg=stg, src=src: e.dma_start(out=stg, in_=src),
               [("sw", 16)])
            ld = CNT["sw"]
            for kc in range(16):
                wait("tensor", "sw", ld)
                tpx(stg[:, kc * 128:(kc + 1) * 128],
                    wT[sl * 3 + wi][:, kc * 128:(kc + 1) * 128], "scalar")
                wprep[("stg", wi % 2)] = CNT["spe"]
        wprep[("ready", hh)] = CNT["sac"]
        msrc = mem_d.rearrange("m h d e -> h d m e")[hh]
        wait("gpsimd", "spe", wprep.get(("memfree", hh % 2), 0))
        op("gpsimd", lambda e, msrc=msrc, sl=sl: e.dma_start(
            out=memh[sl][:, :].rearrange("p (m e) -> p m e", m=4), in_=msrc),
           [("sw", 16)])
        wprep[("memld", hh)] = CNT["sw"]

    if NHEADS > 0:
        prep_head_weights(0)

    pst = {"bk01": [0, 0], "bk01s": ["sdv", "sdv"], "bk23": [0, 0],
           "bk45": [0, 0], "pb1": [0, 0], "outT": [0, 0], "exS": 0}

    for hh in range(NHEADS):
        sl = hh % 2
        wqT, wkT, wvT = wT[sl * 3 + 0], wT[sl * 3 + 1], wT[sl * 3 + 2]
        WREADY = wprep[("ready", hh)]
        MEMLD = wprep[("memld", hh)]

        wait("vector", "sexw", pst["exS"])
        op("vector", lambda e: e.memset(contrib[:, 0:136], 0.0), [("sdv", 1)])
        ZDONE = CNT["sdv"]

        def proj(wTt, out_evac):
            for p in range(4):
                pp = p % 2
                bk = PF[pp]
                wait("tensor", "sac", WREADY)
                wait("tensor", pst["bk01s"][pp], pst["bk01"][pp])
                for kc in range(16):
                    op("tensor", lambda e, bk=bk, wTt=wTt, kc=kc, p=p: e.matmul(
                        bk[:, :], wTt[:, kc * 128:(kc + 1) * 128],
                        BUF_A[:, kc * 2048 + p * 512: kc * 2048 + (p + 1) * 512],
                        start=(kc == 0), stop=(kc == 15)),
                       [("spe", 1)] if kc == 15 else ())
                out_evac(p, bk, pp)

        def k_evac(p, bk, pp):
            mk = CNT["spe"]
            wait("vector", "spe", mk)
            op("vector", lambda e, bk=bk, pp=pp: e.tensor_single_scalar(
                out=p2t[pp], in_=bk[:, :], scalar=0.0, op=ALU.min), [("sdv", 1)])
            wait("scalar", "sdv")
            op("scalar", lambda e, pp=pp: e.activation(out=p2t[pp], in_=p2t[pp],
                                                       func=AF.Exp), [("sac", 1)])
            wait("vector", "sac")
            op("vector", lambda e, bk=bk, pp=pp, p=p: e.scalar_tensor_tensor(
                out=sk_bf[:, p * 512:(p + 1) * 512], in0=bk[:, :], scalar=0.0,
                in1=p2t[pp], op0=ALU.max, op1=ALU.add), [("sdv", 1)])
            pst["bk01s"][pp], pst["bk01"][pp] = "sdv", CNT["sdv"]
        proj(wkT, k_evac)

        def v_evac(p, bk, pp):
            mk = CNT["spe"]
            wait("scalar", "spe", mk)
            op("scalar", lambda e, bk=bk, p=p: e.copy(
                out=v_bf[:, p * 512:(p + 1) * 512], in_=bk[:, :]), [("sac", 1)])
            pst["bk01s"][pp], pst["bk01"][pp] = "sac", CNT["sac"]
        proj(wvT, v_evac)
        KV_DONE = CNT["sac"]

        SKV = CNT["sdv"]
        for c in range(NCH):
            par = c % 2
            wait("tensor", "sdv", SKV)
            wait("tensor", "sac", KV_DONE)
            tpx(sk_bf[:, c * 128:(c + 1) * 128], kpT[:, c * 128:(c + 1) * 128], "vector")
            tpx(v_bf[:, c * 128:(c + 1) * 128], vT[:, c * 128:(c + 1) * 128], "vector")
            pst["pb1"][par] = CNT["sdv"]
            wait("vector", "sdv", ZDONE if c == 0 else CNT["sdv"])
            op("vector", lambda e, c=c: e.tensor_copy(
                out=spv[:, c * 132: c * 132 + 129], in_=contrib[:, 0:129]), [("sdv", 1)])
            SNAP = CNT["sdv"]
            bkv = PF[par]
            wait("tensor", pst["bk01s"][par], pst["bk01"][par])
            wait("tensor", "sdv", pst["pb1"][par])
            op("tensor", lambda e, bkv=bkv, c=c: e.matmul(
                bkv[:, 0:128], kpT[:, c * 128:(c + 1) * 128],
                vT[:, c * 128:(c + 1) * 128], start=True, stop=True), [("spe", 1)])
            op("tensor", lambda e, bkv=bkv, c=c: e.matmul(
                bkv[:, 128:129], kpT[:, c * 128:(c + 1) * 128], onescol,
                start=True, stop=True), [("spe", 1)])
            KVM = CNT["spe"]
            wait("vector", "spe", KVM)
            wait("vector", "sdv", SNAP)
            op("vector", lambda e, bkv=bkv: e.tensor_add(out=contrib[:, 0:129],
                                                         in0=contrib[:, 0:129],
                                                         in1=bkv[:, 0:129]), [("sdv", 1)])
            pst["bk01s"][par], pst["bk01"][par] = "sdv", CNT["sdv"]
        SCAN_DONE = CNT["sdv"]

        wait("gpsimd", "sdv", SCAN_DONE)

        def ex_write_S(e, hh=hh):
            with e.If_eq(regs["pvr"], 0):
                e.dma_start(out=exd[hh, :, 0:130],
                            in_=contrib[:, 0:130]).then_inc(SEM["sexw"], 16)
            with e.Else():
                e.dma_start(out=exd[H + hh, :, 0:130],
                            in_=contrib[:, 0:130]).then_inc(SEM["sexw"], 16)
        if USE_EX:
            em("gpsimd", ex_write_S)
            CNT["sexw"] += 16
        pst["exS"] = CNT["sexw"]

        def q_evac(p, bk, pp):
            mk = CNT["spe"]
            wait("scalar", "spe", mk)
            op("scalar", lambda e, bk=bk, pp=pp, p=p: e.activation(
                out=p2t[pp], in_=bk[:, :], func=AF.Identity,
                accum_out=qsumt[:, p:p + 1]), [("sac", 1)])
            QS = CNT["sac"]
            wait("vector", "sac", QS)
            op("vector", lambda e, bk=bk, pp=pp: e.tensor_single_scalar(
                out=p2t[pp], in_=bk[:, :], scalar=0.0, op=ALU.min), [("sdv", 1)])
            wait("scalar", "sdv")
            op("scalar", lambda e, pp=pp: e.activation(out=p2t[pp], in_=p2t[pp],
                                                       func=AF.Exp), [("sac", 1)])
            wait("vector", "sac")
            op("vector", lambda e, bk=bk, pp=pp, p=p: e.scalar_tensor_tensor(
                out=sq_bf[:, p * 512:(p + 1) * 512], in0=bk[:, :], scalar=0.0,
                in1=p2t[pp], op0=ALU.max, op1=ALU.add), [("sdv", 1)])
            pst["bk01s"][pp], pst["bk01"][pp] = "sdv", CNT["sdv"]
        proj(wqT, q_evac)
        SQ_DONE = CNT["sdv"]

        wait("vector", "sdv", SQ_DONE)
        op("vector", lambda e: e.reduce_sum(out=stat1, in_=qsumt, axis=AX.X), [("sdv", 1)])
        wait("vector", "sdv")
        op("vector", lambda e: e.tensor_copy(out=qsum_bf, in_=stat1), [("sdv", 1)])
        QSB = CNT["sdv"]
        wait("tensor", "sdv", QSB)
        wait("tensor", "sdv", pst["bk23"][0])
        op("tensor", lambda e, hh=hh: e.matmul(PF[2][0:1, 0:4], qsum_bf,
                                               mnT[:, hh:64:16], start=True, stop=True),
           [("spe", 1)])
        wait("vector", "spe")
        op("vector", lambda e: e.tensor_copy(out=contrib[0:1, 130:134],
                                             in_=PF[2][0:1, 0:4]), [("sdv", 1)])
        RELC = CNT["sdv"]

        wait("gpsimd", "sdv", RELC)
        wait("gpsimd", "sexw", CNT["sexw"])

        def ex_write_rel(e, hh=hh):
            with e.If_eq(regs["pvr"], 0):
                e.dma_start(out=exd[hh, 0:1, 130:134],
                            in_=contrib[0:1, 130:134]).then_inc(SEM["sexw"], 16)
            with e.Else():
                e.dma_start(out=exd[H + hh, 0:1, 130:134],
                            in_=contrib[0:1, 130:134]).then_inc(SEM["sexw"], 16)
        if USE_EX:
            em("gpsimd", ex_write_rel)
            CNT["sexw"] += 16
        wait("gpsimd", "sexw", CNT["sexw"])

        def ex_write_flag(e, hh=hh):
            with e.If_eq(regs["pvr"], 0):
                e.dma_start(out=exf[0:1, hh:hh + 1],
                            in_=flagt[0:1, hh:hh + 1]).then_inc(SEM["sexw"], 16)
            with e.Else():
                e.dma_start(out=exf[1:2, hh:hh + 1],
                            in_=flagt[0:1, hh:hh + 1]).then_inc(SEM["sexw"], 16)
        if USE_EX:
            em("gpsimd", ex_write_flag)
            CNT["sexw"] += 16

        wait("tensor", "sdv", RELC)
        for c in range(NCH):
            par = c % 2
            bsc = PF[2 + par]
            bnm = PF[4 + par]
            wait("tensor", "sdv", pst["bk23"][par])
            op("tensor", lambda e, bsc=bsc, c=c: e.matmul(
                bsc[:, 0:128], sk_bf[:, c * 128:(c + 1) * 128],
                sq_bf[:, c * 128:(c + 1) * 128], start=True, stop=True), [("spe", 1)])
            SCM = CNT["spe"]
            wait("vector", "spe", SCM)
            op("vector", lambda e, bsc=bsc, par=par: e.tensor_mul(
                out=scT[par], in0=bsc[:, 0:128], in1=trimask), [("sdv", 1)])
            pst["bk23"][par] = CNT["sdv"]
            SCS = CNT["sdv"]
            wait("tensor", "sdv", pst["bk45"][par])
            wait("tensor", "sdv", SCS)
            op("tensor", lambda e, bnm=bnm, par=par, c=c: e.matmul(
                bnm[:, 0:128], scT[par], vT[:, c * 128:(c + 1) * 128],
                start=True, stop=False), [("spe", 1)])
            op("tensor", lambda e, bnm=bnm, c=c: e.matmul(
                bnm[:, 0:128], sq_bf[:, c * 128:(c + 1) * 128],
                spv[:, c * 132: c * 132 + 128], start=False, stop=True), [("spe", 1)])
            op("tensor", lambda e, bnm=bnm, par=par: e.matmul(
                bnm[:, 128:129], scT[par], onescol, start=True, stop=False), [("spe", 1)])
            op("tensor", lambda e, bnm=bnm, c=c: e.matmul(
                bnm[:, 128:129], sq_bf[:, c * 128:(c + 1) * 128],
                spv[:, c * 132 + 128: c * 132 + 129], start=False, stop=True),
               [("spe", 1)])
            NMM = CNT["spe"]
            wait("vector", "spe", NMM)
            op("vector", lambda e, bnm=bnm, c=c: e.tensor_copy(
                out=numb[:, c * 128:(c + 1) * 128], in_=bnm[:, 0:128]), [("sdv", 1)])
            op("vector", lambda e, bnm=bnm, c=c: e.tensor_copy(
                out=denA[:, c:c + 1], in_=bnm[:, 128:129]), [("sdv", 1)])
            pst["bk45"][par] = CNT["sdv"]

        def ex_poll_read(e, hh=hh):
            fl = regs["fl"]

            def poll_body(src_flag, src_data):
                e.reg_mov(fl, 1)
                with e.While(fl):
                    e.dma_start(out=flagin[0:1, 0:1],
                                in_=src_flag).then_inc(SEM["spoll"], 16)
                    e.nop(cycle_cnt=2000)
                    e.tensor_tensor(out=flagin[0:1, 1:2], in0=flagin[0:1, 0:1],
                                    in1=flagt[0:1, hh:hh + 1], op=ALU.subtract)
                    e.nop(cycle_cnt=400)
                    e.reg_load(fl, flagin[0:1, 1:2].bitcast(I32))
                e.dma_start(out=exin[:, 0:136], in_=src_data).then_inc(
                    SEM["sexr"], 16)

            with e.If_eq(regs["pvr"], 0):
                poll_body(exf[1:2, hh:hh + 1], exd[H + hh, :, :])
            with e.Else():
                poll_body(exf[0:1, hh:hh + 1], exd[hh, :, :])
        if USE_EX:
            em("gpsimd", ex_poll_read)
            CNT["sexr"] += 16
        else:
            op("gpsimd", lambda e: e.dma_start(out=exin[:, 0:136],
                                               in_=exd[hh, :, :]), [("sexr", 16)])
        EXR = CNT["sexr"]

        wait("vector", "sexr", EXR)
        op("vector", lambda e: e.tensor_scalar_mul(out=srem[:, 0:129], in0=exin[:, 0:129],
                                                   scalar1=usest[:, 0:1]), [("sdv", 1)])
        SREMD = CNT["sdv"]
        op("vector", lambda e: e.tensor_add(out=reltot, in0=contrib[0:1, 130:134],
                                            in1=exin[0:1, 130:134]), [("sdv", 1)])
        wait("vector", "sdv")
        op("vector", lambda e: e.scalar_tensor_tensor(out=relsc[0:1, 0:4], in0=reltot,
                                                      scalar=SCALE / 4096.0, in1=bneg,
                                                      op0=ALU.mult, op1=ALU.add),
           [("sdv", 1)])
        wait("vector", "sdv")
        op("vector", lambda e: e.reduce_max(out=relsc[0:1, 4:5], in_=relsc[0:1, 0:4],
                                            axis=AX.X), [("sdv", 1)])
        wait("vector", "sdv")
        op("vector", lambda e: e.tensor_scalar_mul(out=relsc[0:1, 5:6],
                                                   in0=relsc[0:1, 4:5], scalar1=-1.0),
           [("sdv", 1)])
        wait("scalar", "sdv")
        op("scalar", lambda e: e.activation(out=wfin[0:1, 0:4], in_=relsc[0:1, 0:4],
                                            func=AF.Exp, bias=relsc[0:1, 5:6],
                                            accum_out=relsc[0:1, 6:7]), [("sac", 1)])
        wait("vector", "sac")
        op("vector", lambda e: e.reciprocal(out=relsc[0:1, 7:8], in_=relsc[0:1, 6:7]),
           [("sdv", 1)])
        wait("vector", "sdv")
        op("vector", lambda e: e.tensor_scalar_mul(out=wfin[0:1, 0:4], in0=wfin[0:1, 0:4],
                                                   scalar1=relsc[0:1, 7:8]), [("sdv", 1)])
        wait("vector", "sdv")
        op("vector", lambda e: e.tensor_mul(out=wfin[0:1, 0:4], in0=wfin[0:1, 0:4],
                                            in1=em1), [("sdv", 1)])
        wait("vector", "sdv")
        op("vector", lambda e, hh=hh: e.tensor_scalar_mul(
            out=wfin[0:1, 0:4], in0=wfin[0:1, 0:4], scalar1=sigg[0:1, hh:hh + 1]),
           [("sdv", 1)])
        wait("vector", "sdv")
        op("vector", lambda e, hh=hh: e.tensor_copy(out=wfin[0:1, 4:5],
                                                    in_=omg[0:1, hh:hh + 1]), [("sdv", 1)])
        wait("vector", "sdv")
        op("vector", lambda e: e.tensor_copy(out=wfin_bf[0:1, 0:5], in_=wfin[0:1, 0:5]),
           [("sdv", 1)])
        WFB = CNT["sdv"]
        wait("tensor", "sdv", WFB)
        wait("tensor", "sdv", pst["bk23"][0])
        op("tensor", lambda e: e.matmul(PF[2][:, 384:389], onesrow, wfin_bf[0:1, 0:5],
                                        start=True, stop=True), [("spe", 1)])
        wait("vector", "spe")
        op("vector", lambda e: e.tensor_copy(out=wrep[:, 0:5], in_=PF[2][:, 384:389]),
           [("sdv", 1)])
        WREP = CNT["sdv"]
        pst["bk23"][0] = WREP

        for c in range(NCH):
            par = c % 2
            brm = PF[2 + par]
            wait("tensor", "sdv", pst["bk23"][par])
            wait("tensor", "sdv", SREMD)
            op("tensor", lambda e, brm=brm, c=c: e.matmul(
                brm[:, 0:128], sq_bf[:, c * 128:(c + 1) * 128], srem[:, 0:128],
                start=True, stop=True), [("spe", 1)])
            op("tensor", lambda e, brm=brm, c=c: e.matmul(
                brm[:, 128:129], sq_bf[:, c * 128:(c + 1) * 128], srem[:, 128:129],
                start=True, stop=True), [("spe", 1)])
            RMM = CNT["spe"]
            wait("vector", "spe", RMM)
            op("vector", lambda e, brm=brm, c=c: e.tensor_add(
                out=numb[:, c * 128:(c + 1) * 128], in0=numb[:, c * 128:(c + 1) * 128],
                in1=brm[:, 0:128]), [("sdv", 1)])
            op("vector", lambda e, brm=brm, c=c: e.tensor_copy(
                out=denR[:, c:c + 1], in_=brm[:, 128:129]), [("sdv", 1)])
            pst["bk23"][par] = CNT["sdv"]
        B1_DONE = CNT["sdv"]

        wait("vector", "sdv", B1_DONE)
        op("vector", lambda e: e.tensor_add(out=rdent, in0=denA, in1=denR), [("sdv", 1)])
        wait("vector", "sdv")
        op("vector", lambda e: e.tensor_single_scalar(out=rdent, in_=rdent,
                                                      scalar=EPS_AT, op=ALU.max),
           [("sdv", 1)])
        wait("vector", "sdv")
        op("vector", lambda e: e.reciprocal(out=rdent, in_=rdent), [("sdv", 1)])
        wait("vector", "sdv")
        op("vector", lambda e: e.tensor_scalar_mul(out=rdent, in0=rdent,
                                                   scalar1=wrep[:, 4:5]), [("sdv", 1)])
        DEN_DONE = CNT["sdv"]

        wait("tensor", "sdv", pst["bk23"][0])
        wait("tensor", "sdv", pst["bk23"][1])
        for c in range(NCH):
            op("tensor", lambda e, c=c, hh=hh: e.matmul(
                PF[2][:, c * 4:(c + 1) * 4], sq_bf[:, c * 128:(c + 1) * 128],
                mnT[:, hh:64:16], start=True, stop=True),
               [("spe", 1)] if c == NCH - 1 else ())
        NRM = CNT["spe"]
        wait("vector", "spe", NRM)
        op("vector", lambda e: e.tensor_single_scalar(out=sfacall, in_=PF[2][:, 0:64],
                                                      scalar=EPS_AT, op=ALU.max),
           [("sdv", 1)])
        pst["bk23"][0] = CNT["sdv"]
        pst["bk23"][1] = CNT["sdv"]
        wait("vector", "sdv")
        op("vector", lambda e: e.reciprocal(out=sfacall, in_=sfacall), [("sdv", 1)])
        wait("vector", "sdv")
        wait("vector", "sdv", WREP)
        for m in range(4):
            op("vector", lambda e, m=m: e.tensor_scalar_mul(
                out=sfacall[:, m:64:4], in0=sfacall[:, m:64:4],
                scalar1=wrep[:, m:m + 1]), [("sdv", 1)])
        SFAC_DONE = CNT["sdv"]

        for c in range(NCH):
            par = c % 2
            bmm = PF[4 + par]
            wait("tensor", "sdv", pst["bk45"][par])
            wait("tensor", "sw", MEMLD)
            op("tensor", lambda e, bmm=bmm, c=c, sl=sl: e.matmul(
                bmm[:, 0:512], sq_bf[:, c * 128:(c + 1) * 128], memh[sl][:, :],
                start=True, stop=True), [("spe", 1)])
            MMM = CNT["spe"]
            wait("vector", "spe", MMM)
            wait("vector", "sdv", SFAC_DONE)
            wait("vector", "sdv", DEN_DONE)
            if NOLOC:
                op("vector", lambda e: e.memset(p2out, 0.0), [("sdv", 1)])
            else:
                op("vector", lambda e, c=c: e.tensor_scalar_mul(
                    out=p2out, in0=numb[:, c * 128:(c + 1) * 128],
                    scalar1=rdent[:, c:c + 1]), [("sdv", 1)])
            wait("vector", "sdv")
            for m in ([] if NOMEM else range(4)):
                op("vector", lambda e, bmm=bmm, m=m, c=c: e.scalar_tensor_tensor(
                    out=p2out, in0=bmm[:, m * 128:(m + 1) * 128],
                    scalar=sfacall[:, c * 4 + m: c * 4 + m + 1], in1=p2out,
                    op0=ALU.mult, op1=ALU.add), [("sdv", 1)])
                wait("vector", "sdv")
            pst["bk45"][par] = CNT["sdv"]
            op("vector", lambda e, par=par: e.tensor_copy(out=scT[par], in_=p2out),
               [("sdv", 1)])
            CST = CNT["sdv"]
            wait("tensor", "sdv", CST)
            mk = tpx(scT[par], BUF_B[:, hh * 2048 + c * 128: hh * 2048 + (c + 1) * 128],
                     "scalar")
            pst["outT"][par] = mk

        if hh + 1 < NHEADS:
            wprep[("slot", (hh + 1) % 2)] = CNT["spe"]
            wprep[("memfree", (hh + 1) % 2)] = CNT["spe"]
            prep_head_weights(hh + 1)

    if STAGE == "dbg2":
        def dump2(src_ap, t):
            wait("vector", "sdv", CNT["sdv"])
            wait("vector", "sac", CNT["sac"])
            wait("vector", "sy", CNT["sy"])
            op("vector", lambda e, src_ap=src_ap: e.tensor_copy(out=x2t, in_=src_ap),
               [("sdv", 1)])
            wait("sync", "sdv", CNT["sdv"])
            op("sync", lambda e, t=t: e.dma_start(out=y_t[t], in_=x2t), [("sy", 16)])
        for t in range(16):
            dump2(BUF_B[:, t * 2048:(t + 1) * 2048], t)
        wait("sync", "sy", CNT["sy"])
        return nc, Q

    if STAGE == "dbg":
        def dump_bf(src_ap, t):
            wait("vector", "sdv", CNT["sdv"])
            wait("vector", "sac", CNT["sac"])
            wait("vector", "sy", CNT["sy"])
            op("vector", lambda e, src_ap=src_ap: e.tensor_copy(out=x2t, in_=src_ap),
               [("sdv", 1)])
            wait("sync", "sdv", CNT["sdv"])
            op("sync", lambda e, t=t: e.dma_start(out=y_t[t], in_=x2t), [("sy", 16)])
            wait("sync", "sy", CNT["sy"])

        dump_bf(numb[:, :], 0)
        wait("vector", "sy", CNT["sy"])
        op("vector", lambda e: e.memset(x2t, 0.0), [("sdv", 1)])
        wait("vector", "sdv")
        op("vector", lambda e: e.tensor_copy(out=x2t[:, 0:16], in_=denA), [("sdv", 1)])
        op("vector", lambda e: e.tensor_copy(out=x2t[:, 16:32], in_=denR), [("sdv", 1)])
        op("vector", lambda e: e.tensor_copy(out=x2t[:, 32:48], in_=rdent), [("sdv", 1)])
        op("vector", lambda e: e.tensor_copy(out=x2t[:, 48:180], in_=srem[:, 0:132]),
           [("sdv", 1)])
        op("vector", lambda e: e.tensor_copy(out=x2t[:, 180:316], in_=contrib),
           [("sdv", 1)])
        wait("sync", "sdv", CNT["sdv"])
        op("sync", lambda e: e.dma_start(out=y_t[1], in_=x2t), [("sy", 16)])
        dump_bf(sq_bf[:, :], 2)
        dump_bf(sk_bf[:, :], 3)
        dump_bf(v_bf[:, :], 4)
        dump_bf(spv[:, 0:2048], 5)
        dump_bf(kpT[:, :], 6)
        dump_bf(vT[:, :], 7)
        DH = int(os.environ.get("MK_DH", "0"))
        dump_bf(BUF_B[:, DH * 2048:(DH + 1) * 2048], 8)
        dump_bf(BUF_A[:, 0:2048], 9)
        wait("sync", "sy", CNT["sy"])
        return nc, Q

    barrier()

    if STAGE == "p2":
        wait("sync", "sacc", CNT["sacc"])
        for t in range(16):
            op("sync", lambda e, t=t: e.dma_start(out=x2t, in_=x1_t[t]), [("sx", 16)])
            wait("sync", "sx")
            op("sync", lambda e, t=t: e.dma_start(out=y_t[t], in_=x2t), [("sy", 16)])
        wait("sync", "sy", CNT["sy"])
        return nc, Q

    wo_r = wo_d.rearrange("(ec p) (h dd) -> h p ec dd", p=128, dd=128)
    wst3 = {"stg": 0, "tp": [0, 0]}
    for h2 in range(H):
        wait("gpsimd", "spe", wst3["stg"])
        op("gpsimd", lambda e, h2=h2: e.dma_start(
            out=wostg[:, :].rearrange("p (ec dd) -> p ec dd", ec=16), in_=wo_r[h2]),
           [("sw", 16)])
        WLD = CNT["sw"]
        for ec in range(16):
            wait("tensor", "sw", WLD)
            tpx(wostg[:, ec * 128:(ec + 1) * 128],
                woT[:, h2 * 2048 + ec * 128: h2 * 2048 + (ec + 1) * 128], "scalar")
            wst3["stg"] = CNT["spe"]
    WOT_DONE = CNT["sac"]

    if STAGE == "dbgwo":
        def dump3(src_ap, t):
            wait("vector", "sdv", CNT["sdv"])
            wait("vector", "sac", CNT["sac"])
            wait("vector", "sy", CNT["sy"])
            op("vector", lambda e, src_ap=src_ap: e.tensor_copy(out=x2t, in_=src_ap),
               [("sdv", 1)])
            wait("sync", "sdv", CNT["sdv"])
            op("sync", lambda e, t=t: e.dma_start(out=y_t[t], in_=x2t), [("sy", 16)])
        for t in range(16):
            dump3(woT[:, t * 2048:(t + 1) * 2048], t)
        wait("sync", "sy", CNT["sy"])
        return nc, Q

    p3 = {"evac": [0, 0], "x2free": 0}
    for t in range(16):
        if lnst["ln"]:
            wait("sync", "sdv", lnst["ln"][-1])
        wait("sync", "sy", p3["x2free"])
        op("sync", lambda e, t=t: e.dma_start(out=x2t, in_=x1_t[t]), [("sx", 16)])
        XLD = CNT["sx"]
        for es in range(4):
            par = es % 2
            bk = PF[par]
            wait("tensor", "sdv", p3["evac"][par])
            wait("tensor", "sac", WOT_DONE)
            for h2 in range(16):
                op("tensor", lambda e, bk=bk, h2=h2, t=t, es=es: e.matmul(
                    bk[:, :], BUF_B[:, h2 * 2048 + t * 128: h2 * 2048 + (t + 1) * 128],
                    woT[:, h2 * 2048 + es * 512: h2 * 2048 + (es + 1) * 512],
                    start=(h2 == 0), stop=(h2 == 15)),
                   [("spe", 1)] if h2 == 15 else ())
            PM = CNT["spe"]
            wait("vector", "spe", PM)
            wait("vector", "sx", XLD)
            op("vector", lambda e, bk=bk, es=es: e.tensor_add(
                out=x2t[:, es * 512:(es + 1) * 512],
                in0=x2t[:, es * 512:(es + 1) * 512], in1=bk[:, :]), [("sdv", 1)])
            p3["evac"][par] = CNT["sdv"]
        X2DONE = CNT["sdv"]
        wait("sync", "sdv", X2DONE)
        op("sync", lambda e, t=t: e.dma_start(out=y_t[t], in_=x2t), [("sy", 16)])
        p3["x2free"] = CNT["sy"]
        ln_tile(x2t, lambda kc, t=t: BUF_A[:, kc * 2048 + t * 128:
                                          kc * 2048 + (t + 1) * 128],
                xln3sc, ready=("sdv", X2DONE))

    barrier()
    wait("gpsimd", "sy", CNT["sy"])

    if STAGE == "p3":
        wait("sync", "sy", CNT["sy"])
        return nc, Q

    ffn_pass(4)

    wait("sync", "sacc", CNT["sacc"])
    wait("sync", "sy", CNT["sy"])
    return nc, Q


def _assemble(nc, Q):
    with nc.Block() as block:
        block.sync(lambda e: [f(e) for f in Q["sync"]])
        block.tensor(lambda e: [f(e) for f in Q["tensor"]])
        block.vector(lambda e: [f(e) for f in Q["vector"]])
        block.scalar(lambda e: [f(e) for f in Q["scalar"]])
        block.gpsimd(lambda e: [f(e) for f in Q["gpsimd"]])
    return nc


_CACHE = {}


def _get_nc():
    if STAGE not in _CACHE:
        nc, Q = build()
        _CACHE[STAGE] = _assemble(nc, Q)
    return _CACHE[STAGE]


def kernel(L0_attn_hidden_states, L0_attn_memories, L0_attn_memory_norms,
           L0_attn_w_q, L0_attn_w_k, L0_attn_w_v, L0_attn_w_o, L0_attn_gate,
           L0_ffn_w_gate, L0_ffn_w_up, L0_ffn_w_down):
    import ml_dtypes
    nc = _get_nc()
    f32 = np.float32
    x = np.ascontiguousarray(np.asarray(L0_attn_hidden_states, dtype=f32))
    args = {
        "wq": np.ascontiguousarray(np.asarray(L0_attn_w_q, dtype=f32)),
        "wk": np.ascontiguousarray(np.asarray(L0_attn_w_k, dtype=f32)),
        "wv": np.ascontiguousarray(np.asarray(L0_attn_w_v, dtype=f32)),
        "wo": np.ascontiguousarray(np.asarray(L0_attn_w_o, dtype=f32)),
        "wg_in": np.ascontiguousarray(np.asarray(L0_ffn_w_gate, dtype=f32)),
        "wu_in": np.ascontiguousarray(np.asarray(L0_ffn_w_up, dtype=f32)),
        "wd_in": np.ascontiguousarray(np.asarray(L0_ffn_w_down, dtype=f32)),
        "mem_in": np.ascontiguousarray(np.asarray(L0_attn_memories, dtype=f32)),
        "mn_in": np.ascontiguousarray(np.asarray(L0_attn_memory_norms, dtype=f32)),
        "gate_in": np.ascontiguousarray(
            np.asarray(L0_attn_gate, dtype=f32)).reshape(1, H),
        "trimask_in": np.triu(np.ones((128, 128))).astype(ml_dtypes.bfloat16),
        "identb_in": np.eye(128).astype(ml_dtypes.bfloat16),
        "onescol_in": np.ones((128, 1), dtype=ml_dtypes.bfloat16),
        "onesrow_in": np.ones((1, 128), dtype=ml_dtypes.bfloat16),
    }
    nonce = int(np.random.randint(1, 2 ** 20))
    flagv = np.zeros((1, 24), dtype=np.float32)
    flagv[0, :16] = (nonce + 1 + np.arange(16)).astype(np.float32)
    in_maps = []
    for c in range(8):
        b, half = c // 2, c % 2
        m = dict(args)
        m["x_in"] = np.ascontiguousarray(x[b, half * TOK:(half + 1) * TOK, :])
        m["pidx_in"] = np.array([[half, 1 - half]], dtype=np.int32)
        m["usest_in"] = np.full((128, 1), float(half), dtype=f32)
        m["flagv_in"] = flagv
        in_maps.append(m)
    res = run_bass_kernel_spmd(nc, in_maps, core_ids=list(range(8)))
    out = np.empty((4, 4096, D), dtype=f32)
    for c in range(8):
        b, half = c // 2, c % 2
        out[b, half * TOK:(half + 1) * TOK, :] = res.results[c]["y"]
    return out


# version 03
# speedup vs baseline: 1.2019x; 1.2019x over previous
import os
import numpy as np
import concourse.bass as bass
import concourse.mybir as mybir
from concourse.bass_utils import run_bass_kernel_spmd

F32 = mybir.dt.float32
BF16 = mybir.dt.bfloat16
I32 = mybir.dt.int32
AF = mybir.ActivationFunctionType
ALU = mybir.AluOpType
AX = mybir.AxisListType
ET = mybir.EngineType

D = 2048
TOK = 2048
H = 16
HD = 128
M = 4
NCH = 16
IF = 8192
NMEGA = 2
EPS_LN = 1e-5
EPS_AT = 1e-6
SCALE = HD ** -0.5

STAGE = os.environ.get("MK_STAGE", "full")
NQR = int(os.environ.get("MK_NQ", "4"))
LNLVL = int(os.environ.get("MK_LN", "4"))
NOSPARS = os.environ.get("MK_NOSPARS", "0") == "1"
NHEADS = int(os.environ.get("MK_NH", "16"))
USE_EX = os.environ.get("MK_EX", "1") == "1"
NOMEM = os.environ.get("MK_NOMEM", "0") == "1"
NOLOC = os.environ.get("MK_NOLOC", "0") == "1"

OFF_A = 20448
OFF_B = 85984
DYN0 = 151520

ENGS = ("sync", "tensor", "vector", "scalar", "gpsimd")


def build():
    nc = bass.Bass("TRN2", target_bir_lowering=False, monotonic_sem_count=0)
    Q = {e: [] for e in ENGS}
    CNT = {}
    SEM = {}

    def sem(name):
        SEM[name] = nc.alloc_semaphore(name)
        CNT[name] = 0

    def em(eng, fn):
        Q[eng].append(fn)

    WMAX = {e: {} for e in ENGS}

    def wait(eng, name, val=None):
        v = CNT[name] if val is None else val
        if v <= 0 or WMAX[eng].get(name, 0) >= v:
            return
        WMAX[eng][name] = v
        em(eng, lambda e, h=SEM[name], v=v: e.wait_ge(h, v))

    def op(eng, makefn, incs=()):
        hs = []
        for s, k in incs:
            CNT[s] += k
            hs.append((SEM[s], k))

        def run(e, makefn=makefn, hs=hs):
            ins = makefn(e)
            for h, k in hs:
                ins = ins.then_inc(h, k)
        em(eng, run)

    bar_i = [0]

    def barrier():
        bar_i[0] += 1
        s = f"bar{bar_i[0]}"
        sem(s)
        for e in ENGS:
            op(e, lambda en, h=SEM[s]: en.sem_inc(h, 1))
        CNT[s] = 5
        for e in ENGS:
            wait(e, s, 5)

    dt_ = nc.dram_tensor
    x_in = dt_("x_in", [TOK, D], F32, kind="ExternalInput").ap()
    wq_d = dt_("wq", [D, D], F32, kind="ExternalInput").ap()
    wk_d = dt_("wk", [D, D], F32, kind="ExternalInput").ap()
    wv_d = dt_("wv", [D, D], F32, kind="ExternalInput").ap()
    wo_d = dt_("wo", [D, D], F32, kind="ExternalInput").ap()
    wg_d = dt_("wg_in", [D, IF], F32, kind="ExternalInput").ap()
    wu_d = dt_("wu_in", [D, IF], F32, kind="ExternalInput").ap()
    wd_d = dt_("wd_in", [IF, D], F32, kind="ExternalInput").ap()
    mem_d = dt_("mem_in", [M, H, HD, HD], F32, kind="ExternalInput").ap()
    mn_d = dt_("mn_in", [M, H, HD], F32, kind="ExternalInput").ap()
    gate_d = dt_("gate_in", [1, H], F32, kind="ExternalInput").ap()
    trim_d = dt_("trimask_in", [128, 128], BF16, kind="ExternalInput").ap()
    idb_d = dt_("identb_in", [128, 128], BF16, kind="ExternalInput").ap()
    onc_d = dt_("onescol_in", [128, 1], BF16, kind="ExternalInput").ap()
    onr_d = dt_("onesrow_in", [1, 128], BF16, kind="ExternalInput").ap()
    ust_d = dt_("usest_in", [128, 1], F32, kind="ExternalInput").ap()
    pid_d = dt_("pidx_in", [1, 2], I32, kind="ExternalInput").ap()
    flg_d = dt_("flagv_in", [1, 24], F32, kind="ExternalInput").ap()

    y = dt_("y", [TOK, D], F32, kind="ExternalOutput").ap()
    x1s = dt_("x1_scr", [TOK, D], F32, kind="Internal").ap()
    wgsc = dt_("wg_sc", [16, 128, 8192], BF16, kind="Internal").ap()
    wusc = dt_("wu_sc", [16, 128, 8192], BF16, kind="Internal").ap()
    wdsc = dt_("wd_sc", [4, 128, 32768], BF16, kind="Internal").ap()
    exd = dt_("exd", [2 * H, 128, 136], F32, kind="Internal",
              addr_space="Shared").ap()
    exf = dt_("exf", [2, 32], F32, kind="Internal", addr_space="Shared").ap()

    A_ = nc.alloc_sbuf_tensor
    trimask = A_("trimask", [128, 128], BF16).ap()
    identb = A_("identb", [128, 128], BF16).ap()
    onescol = A_("onescol", [128, 1], BF16).ap()
    onesrow = A_("onesrow", [1, 128], BF16).ap()
    usest = A_("usest", [128, 1], F32).ap()
    pidxt = A_("pidxt", [1, 2], I32).ap()
    flagt = A_("flagt", [1, 24], F32).ap()
    flagin = A_("flagin", [1, 8], F32).ap()
    gatet = A_("gatet", [1, H], F32).ap()
    sigg = A_("sigg", [1, H], F32).ap()
    omg = A_("omg", [1, H], F32).ap()
    etile = A_("etile", [1, 4], F32).ap()
    bneg = A_("bneg", [1, 4], F32).ap()
    em1 = A_("em1", [1, 4], F32).ap()
    mnT = A_("mnT", [128, 64], BF16).ap()
    contrib = A_("contrib", [128, 136], F32).ap()
    exin = A_("exin", [128, 136], F32).ap()
    srem = A_("srem", [128, 132], BF16).ap()
    stat1 = A_("stat1", [128, 1], F32).ap()
    stat2 = A_("stat2", [128, 1], F32).ap()
    stat3 = A_("stat3", [128, 1], F32).ap()
    qsumt = A_("qsumt", [128, 4], F32).ap()
    qsum_bf = A_("qsum_bf", [128, 1], BF16).ap()
    reltot = A_("reltot", [1, 4], F32).ap()
    relsc = A_("relsc", [1, 8], F32).ap()
    wfin = A_("wfin", [1, 8], F32).ap()
    wfin_bf = A_("wfin_bf", [1, 8], BF16).ap()
    wrep = A_("wrep", [128, 8], F32).ap()
    denA = A_("denA", [128, 16], F32).ap()
    denR = A_("denR", [128, 16], F32).ap()
    rdent = A_("rdent", [128, 16], F32).ap()

    _pr = A_("pr_probe", [128, 8], F32)
    _pa = nc.lookup_mloc(_pr).addr
    assert _pa + 32 <= OFF_A, f"persistent SBUF region overflows into BUF_A: {_pa}"

    def at(name, shape, dtype, off):
        assert off + int(np.prod(shape[1:])) * (2 if dtype == BF16 else 4) <= 229344, name
        return nc.alloc_sbuf_tensor_at(name, shape, dtype, offset=off).ap()

    BUF_A = at("BUF_A", [128, 16 * 2048], BF16, OFF_A)
    BUF_B = at("BUF_B", [128, 16 * 2048], BF16, OFF_B)

    wgq = [at(f"wgq{i}", [128, 16 * 512], BF16, DYN0 + 16384 * i) for i in range(2)]
    wuq = at("wuq", [128, 16 * 512], BF16, DYN0 + 32768)
    xt0 = at("xt0", [128, 2048], F32, DYN0 + 49152)
    tacc = at("tacc", [128, 2048], F32, DYN0 + 49152)
    xlnsc = at("xlnsc", [128, 2048], BF16, DYN0 + 57344)
    hgate = at("hgate", [128, 512], F32, DYN0 + 61440)
    hgateB = at("hgateB", [128, 512], F32, DYN0 + 73728)
    spa = [at(f"spa{i}", [128, 512], BF16, DYN0 + 63488 + 1024 * i) for i in range(4)]
    spc = [at(f"spc{i}", [128, 512], BF16, DYN0 + 67584 + 1024 * i) for i in range(6)]
    wdq1 = at("wdq1", [128, 16 * 2048], BF16, OFF_A)

    wT = [at(f"wT{i}", [128, 2048], BF16, DYN0 + 4096 * i) for i in range(6)]
    wstg = [at(f"wstg{i}", [128, 2048], BF16, DYN0 + 24576 + 4096 * i) for i in range(2)]
    sq_bf = at("sq_bf", [128, 2048], BF16, DYN0 + 32768)
    sk_bf = at("sk_bf", [128, 2048], BF16, DYN0 + 36864)
    v_bf = at("v_bf", [128, 2048], BF16, DYN0 + 40960)
    kpT = at("kpT", [128, 2048], BF16, DYN0 + 45056)
    vT = at("vT", [128, 2048], BF16, DYN0 + 49152)
    spv = at("spv", [128, 16 * 132], BF16, DYN0 + 53248)
    numb = at("numb", [128, 16 * 128], F32, DYN0 + 57600)
    memh = [at(f"memh{i}", [128, 512], BF16, DYN0 + 65792 + 1024 * i) for i in range(2)]
    p2t = [at(f"p2t{i}", [128, 512], F32, DYN0 + 67840 + 2048 * i) for i in range(2)]
    p2out = at("p2out", [128, 128], F32, DYN0 + 71936)
    scT = [at(f"scT{i}", [128, 128], BF16, DYN0 + 72448 + 256 * i) for i in range(2)]
    sfacall = at("sfacall", [128, 64], F32, DYN0 + 72960)
    x1t2 = at("x1t2", [128, 2048], F32, DYN0 + 32768)
    xln2sc = at("xln2sc", [128, 2048], BF16, DYN0 + 45056)

    woT = at("woT", [128, 16 * 2048], BF16, DYN0 + 0)
    wostg = at("wostg", [128, 2048], BF16, DYN0 + 65536)
    x2t = at("x2t", [128, 2048], F32, DYN0 + 69632)
    xln3sc = at("xln3sc", [128, 2048], BF16, DYN0 + 65536)

    wdh4 = at("wdh4", [128, 8 * 2048], BF16, DYN0 + 0)
    wuq4 = at("wuq4", [128, 16 * 512], BF16, DYN0 + 32768)
    hgate4 = at("hgate4", [128, 512], F32, DYN0 + 49152)
    hgate4B = at("hgate4B", [128, 512], F32, DYN0 + 69632)
    spa4 = [at(f"spa4_{i}", [128, 512], BF16, DYN0 + 51200 + 1024 * i) for i in range(4)]
    spc4 = [at(f"spc4_{i}", [128, 512], BF16, DYN0 + 55296 + 1024 * i) for i in range(6)]
    tacc4 = at("tacc4", [128, 2048], F32, DYN0 + 61440)
    wgq4 = [at(f"wgq4_{i}", [128, 16 * 512], BF16, OFF_B + 32768 + 16384 * i)
            for i in range(2)]

    PF = [nc.alloc_psum_tensor(f"PF{i}", [128, 512], F32).ap() for i in range(6)]
    PB1 = nc.alloc_psum_tensor("PB1", [128, 1024], BF16).ap()
    PB2 = nc.alloc_psum_tensor("PB2", [128, 1024], BF16).ap()

    for s in ("sx", "sw", "swu", "swd", "swb", "swbd", "sacc", "spe", "sdv",
              "sac", "sexw", "spoll", "sexr", "sy"):
        sem(s)

    x_t = x_in.rearrange("(t p) d -> t p d", p=128)
    x1_t = x1s.rearrange("(t p) d -> t p d", p=128)
    y_t = y.rearrange("(t p) d -> t p d", p=128)

    regs = {}

    TP = {"idx": 0, "busy": [None, None]}
    ESEM = {"scalar": "sac", "vector": "sdv"}

    def tpx(src_ap, dst_ap, evac_eng="scalar", ncols=128, ident=None):
        b = TP["idx"] % 2
        TP["idx"] += 1
        bank = PB1 if b == 0 else PB2
        if TP["busy"][b] is not None:
            s, v = TP["busy"][b]
            wait("tensor", s, v)
        idn = identb if ident is None else ident
        op("tensor", lambda e, bank=bank, src_ap=src_ap, idn=idn, ncols=ncols:
           e.transpose(bank[0:src_ap.shape[1], 0:ncols], src_ap, idn), [("spe", 1)])
        m = CNT["spe"]
        wait(evac_eng, "spe", m)
        sname = ESEM[evac_eng]
        if evac_eng == "scalar":
            op(evac_eng, lambda e, dst_ap=dst_ap, bank=bank, src_ap=src_ap, ncols=ncols:
               e.copy(out=dst_ap, in_=bank[0:src_ap.shape[1], 0:ncols]), [(sname, 1)])
        else:
            op(evac_eng, lambda e, dst_ap=dst_ap, bank=bank, src_ap=src_ap, ncols=ncols:
               e.tensor_copy(out=dst_ap, in_=bank[0:src_ap.shape[1], 0:ncols]),
               [(sname, 1)])
        TP["busy"][b] = (sname, CNT[sname])
        return CNT[sname]

    for dst, src in ((trimask, trim_d[:, :]), (identb, idb_d[:, :]),
                     (onescol, onc_d[:, :]), (onesrow, onr_d[:, :]),
                     (usest, ust_d[:, :]), (pidxt, pid_d[:, :]),
                     (flagt, flg_d[:, :]), (gatet, gate_d[:, :])):
        op("sync", lambda e, dst=dst, src=src: e.dma_start(out=dst, in_=src),
           [("sx", 16)])
    mn64 = mn_d.rearrange("m h d -> (m h) d")
    mn4 = mn_d.rearrange("m h d -> m (h d)")
    op("sync", lambda e: e.dma_start(out=xt0[0:64, 0:128], in_=mn64[:, :]), [("sx", 16)])
    op("sync", lambda e: e.dma_start(out=xt0[64:68, 0:2048], in_=mn4[:, :]), [("sx", 16)])
    wait("scalar", "sx")
    op("scalar", lambda e: e.activation(out=sigg, in_=gatet, func=AF.Sigmoid), [("sac", 1)])
    wait("vector", "sac")
    op("vector", lambda e: e.tensor_scalar(out=omg, in0=sigg, scalar1=-1.0, scalar2=1.0,
                                           op0=ALU.mult, op1=ALU.add), [("sdv", 1)])
    wait("vector", "sx")
    op("vector", lambda e: e.tensor_copy(out=spa[0][0:64, 0:128], in_=xt0[0:64, 0:128]),
       [("sdv", 1)])
    wait("tensor", "sdv")
    tpx(spa[0][0:64, 0:128], mnT, "vector", ncols=64, ident=identb[0:64, 0:64])
    op("vector", lambda e: e.reduce_sum(out=stat1[0:4, 0:1], in_=xt0[64:68, 0:2048],
                                        axis=AX.X), [("sdv", 1)])
    wait("vector", "sdv")
    op("vector", lambda e: e.tensor_copy(out=spa[1][0:4, 0:1], in_=stat1[0:4, 0:1]),
       [("sdv", 1)])
    wait("tensor", "sdv")
    tpx(spa[1][0:4, 0:1], relsc[0:1, 4:8], "vector", ncols=4, ident=identb[0:4, 0:4])
    wait("vector", "sdv")
    op("vector", lambda e: e.tensor_single_scalar(out=etile, in_=relsc[0:1, 4:8],
                                                  scalar=1e-6, op=ALU.is_lt), [("sdv", 1)])
    wait("vector", "sdv")
    op("vector", lambda e: e.tensor_scalar_mul(out=bneg, in0=etile, scalar1=-1e30),
       [("sdv", 1)])
    op("vector", lambda e: e.tensor_scalar(out=em1, in0=etile, scalar1=-1.0, scalar2=1.0,
                                           op0=ALU.mult, op1=ALU.add), [("sdv", 1)])
    P0_DV = CNT["sdv"]
    P0_PE = CNT["spe"]
    wait("gpsimd", "sx")

    def _loadregs(e):
        regs["pvr"] = e.alloc_register("pvr")
        regs["fl"] = e.alloc_register("flpoll")
        e.reg_load(regs["pvr"], pidxt[0:1, 0:1])
    em("gpsimd", _loadregs)

    lnst = {"ln": [], "tp": []}

    def ln_tile(xsrc, dst_col_fn, sq_scr, ready=None):
        if LNLVL < 2:
            return CNT["sac"]
        if ready is not None:
            wait("vector", ready[0], ready[1])
        op("vector", lambda e, xsrc=xsrc: e.reduce_sum(out=stat1, in_=xsrc, axis=AX.X),
           [("sdv", 1)])
        wait("vector", "sdv")
        op("vector", lambda e: e.tensor_scalar_mul(out=stat1, in0=stat1, scalar1=-1.0 / D),
           [("sdv", 1)])
        wait("scalar", "sdv")
        op("scalar", lambda e, xsrc=xsrc, sq_scr=sq_scr: e.activation(
            out=sq_scr, in_=xsrc, func=AF.Square, bias=stat1[:, 0:1], scale=1.0,
            accum_out=stat2[:, 0:1]), [("sac", 1)])
        wait("vector", "sac")
        op("vector", lambda e: e.tensor_scalar(out=stat3, in0=stat2, scalar1=1.0 / D,
                                               scalar2=EPS_LN, op0=ALU.mult, op1=ALU.add),
           [("sdv", 1)])
        wait("scalar", "sdv")
        op("scalar", lambda e: e.sqrt(out=stat3, in_=stat3), [("sac", 1)])
        wait("vector", "sac")
        op("vector", lambda e: e.reciprocal(out=stat3, in_=stat3), [("sdv", 1)])
        if LNLVL < 3:
            lnst["ln"].append(CNT["sdv"])
            return CNT["sac"]
        wait("scalar", "sdv")
        op("scalar", lambda e, xsrc=xsrc, sq_scr=sq_scr: e.activation(
            out=sq_scr, in_=xsrc, func=AF.Identity, bias=stat1[:, 0:1], scale=1.0),
           [("sac", 1)])
        wait("vector", "sac")
        op("vector", lambda e, sq_scr=sq_scr: e.tensor_scalar_mul(
            out=sq_scr, in0=sq_scr, scalar1=stat3[:, 0:1]), [("sdv", 1)])
        lnst["ln"].append(CNT["sdv"])
        if LNLVL < 4:
            return CNT["sac"]
        for kc in range(16):
            wait("tensor", "sdv", lnst["ln"][-1])
            mk = tpx(sq_scr[:, kc * 128:(kc + 1) * 128], dst_col_fn(kc), "scalar")
            lnst["tp"].append(mk)
        return CNT["sac"]

    ffst = {"acc_tile": {}, "acc_busy": {}, "wg_wb": [0, 0], "wu_wb": 0,
            "wd_wb": 0}

    def ffn_pass(phase):
        if phase == 1:
            hq, hq_base = BUF_B, 16384
            wgb, wub, wdb = wgq, wuq, wdq1
            spa_, spc_, hg_, tac_ = spa, spc, (hgate, hgateB), tacc
            out_tiles, nspan = x1_t, 4
        else:
            hq, hq_base = BUF_B, 0
            wgb, wub, wdb = wgq4, wuq4, wdh4
            spa_, spc_, hg_, tac_ = spa4, spc4, (hgate4, hgate4B), tacc4
            out_tiles, nspan = y_t, 2

        wg_r = wg_d.rearrange("(q kc p) (g i) -> g q p kc i", q=4, kc=4, p=128,
                              i=512)
        wu_r = wu_d.rearrange("(q kc p) (g i) -> g q p kc i", q=4, kc=4, p=128,
                              i=512)
        wd_r = wd_d.rearrange("(s g c m) e -> s c g m e", g=nspan, c=128, m=4)

        st = {"wg_used": [0, 0], "wu_used": 0, "wd_used": 0,
              "bk_g": [0, 0], "bk_u": [0, 0], "wd_evac": [0, 0], "spars": 0}

        def load_wg(gq, wslot, use_scr):
            marks = []
            if not use_scr:
                wait("gpsimd", "spe", st["wg_used"][wslot])
                wait("gpsimd", "swb", ffst["wg_wb"][wslot])
                for qq in range(4):
                    op("gpsimd", lambda e, gq=gq, wslot=wslot, wgb=wgb, qq=qq:
                       e.dma_start(
                           out=wgb[wslot][:, qq * 2048:(qq + 1) * 2048]
                           .rearrange("p (kc i) -> p kc i", kc=4),
                           in_=wg_r[gq, qq]), [("sw", 16)])
                    marks.append(("sw", CNT["sw"]))
                wait("sync", "sw", CNT["sw"])
                op("sync", lambda e, gq=gq, wslot=wslot, wgb=wgb: e.dma_start(
                    out=wgsc[gq], in_=wgb[wslot][:, :]), [("swb", 16)])
                ffst["wg_wb"][wslot] = CNT["swb"]
            else:
                wait("sync", "spe", st["wg_used"][wslot])
                for qq in range(4):
                    op("sync", lambda e, gq=gq, wslot=wslot, wgb=wgb, qq=qq:
                       e.dma_start(
                           out=wgb[wslot][:, qq * 2048:(qq + 1) * 2048],
                           in_=wgsc[gq][:, qq * 2048:(qq + 1) * 2048]),
                       [("swu", 16)])
                    marks.append(("swu", CNT["swu"]))
            return marks

        def load_wu(gq, use_scr):
            marks = []
            if not use_scr:
                wait("gpsimd", "spe", st["wu_used"])
                wait("gpsimd", "swb", ffst["wu_wb"])
                for qq in range(4):
                    op("gpsimd", lambda e, gq=gq, wub=wub, qq=qq: e.dma_start(
                        out=wub[:, qq * 2048:(qq + 1) * 2048]
                        .rearrange("p (kc i) -> p kc i", kc=4),
                        in_=wu_r[gq, qq]), [("sw", 16)])
                    marks.append(("sw", CNT["sw"]))
                wait("sync", "sw", CNT["sw"])
                op("sync", lambda e, gq=gq, wub=wub: e.dma_start(
                    out=wusc[gq], in_=wub[:, :]), [("swb", 16)])
                ffst["wu_wb"] = CNT["swb"]
            else:
                wait("scalar", "spe", st["wu_used"])
                wait("scalar", "swb", ffst["wu_wb"])
                for qq in range(4):
                    op("scalar", lambda e, gq=gq, wub=wub, qq=qq: e.dma_start(
                        out=wub[:, qq * 2048:(qq + 1) * 2048],
                        in_=wusc[gq][:, qq * 2048:(qq + 1) * 2048]), [("swd", 16)])
                    marks.append(("swd", CNT["swd"]))
            return marks

        def load_wd(sidx, use_scr):
            wait("gpsimd", "spe", st["wd_used"])
            wait("gpsimd", "swbd", ffst["wd_wb"])
            if not use_scr:
                op("gpsimd", lambda e, sidx=sidx, wdb=wdb, nspan=nspan: e.dma_start(
                    out=wdb[:, :].rearrange("p (g m e) -> p g m e", g=nspan, m=4),
                    in_=wd_r[sidx]), [("sw", 16)])
                mark = ("sw", CNT["sw"])
                wait("scalar", "sw", CNT["sw"])
                op("scalar", lambda e, sidx=sidx, wdb=wdb: e.dma_start(
                    out=wdsc[sidx], in_=wdb[:, :]), [("swbd", 16)])
                ffst["wd_wb"] = CNT["swbd"]
            else:
                if phase == 1:
                    src = wdsc[sidx]
                else:
                    src = wdsc[sidx // 2][:, (sidx % 2) * 16384:
                                          (sidx % 2 + 1) * 16384]
                op("gpsimd", lambda e, src=src, wdb=wdb: e.dma_start(
                    out=wdb[:, :], in_=src), [("sw", 16)])
                mark = ("sw", CNT["sw"])
            return mark

        if phase == 4:
            for e_ in ("sync", "scalar", "gpsimd"):
                wait(e_, "swb", CNT["swb"])
                wait(e_, "swbd", CNT["swbd"])

        for mg in range(NMEGA):
            if phase == 1:
                for i in range(8):
                    t = mg * 8 + i
                    if lnst["ln"]:
                        wait("sync", "sdv", lnst["ln"][-1])
                    else:
                        wait("sync", "sdv", P0_DV)
                    if mg > 0 and i == 0:
                        wait("sync", "sacc", CNT["sacc"])
                    op("sync", lambda e, t=t: e.dma_start(out=xt0, in_=x_t[t]),
                       [("sx", 16)])
                    wait("sync", "sx")
                    op("sync", lambda e, t=t: e.dma_start(out=x1_t[t], in_=xt0),
                       [("sacc", 16)])
                    ffst["acc_tile"][(1, t)] = CNT["sacc"]
                    ln_tile(xt0, lambda kc, i=i: BUF_B[:, kc * 1024 + i * 128:
                                                       kc * 1024 + (i + 1) * 128],
                            xlnsc, ready=("sx", CNT["sx"]))
                if i == 7:
                    st["bk_g"] = [max(st["bk_g"][0], CNT["sac"]),
                                  max(st["bk_g"][1], CNT["sac"])]
            use_scr = (phase != 1) or (mg > 0)
            for Qr in range(NQR):
                nhalf = 4 // nspan
                wd_lds = []
                if use_scr:
                    wd_lds.append(load_wd(Qr * nhalf, use_scr))
                for g in range(4):
                    gq = Qr * 4 + g
                    wslot = gq % 2
                    wg_q = load_wg(gq, wslot, use_scr)
                    wu_q = load_wu(gq, use_scr)
                    for p in range(2):
                        for m in range(4):
                            mp = m % 2
                            gbk = PF[4 + mp]
                            ubk = PF[2 + mp]
                            wait("tensor", "sac", st["bk_g"][mp])
                            for kc in range(16):
                                wait("tensor", wg_q[kc // 4][0], wg_q[kc // 4][1])
                                if phase == 1:
                                    rv = hq[:, kc * 1024 + p * 512:
                                            kc * 1024 + (p + 1) * 512]
                                else:
                                    rv = BUF_A[:, kc * 2048 + mg * 1024 + p * 512:
                                               kc * 2048 + mg * 1024 + (p + 1) * 512]
                                lv = wgb[wslot][:, kc * 512 + m: (kc + 1) * 512: 4]
                                op("tensor", lambda e, gbk=gbk, lv=lv, rv=rv, kc=kc:
                                   e.matmul(gbk[:, :], lv, rv, start=(kc == 0),
                                            stop=(kc == 15)),
                                   [("spe", 1)] if kc == 15 else ())
                            st["wg_used"][wslot] = CNT["spe"]
                            gmark = CNT["spe"]
                            wait("tensor", "sdv", st["bk_u"][mp])
                            for kc in range(16):
                                wait("tensor", wu_q[kc // 4][0], wu_q[kc // 4][1])
                                if phase == 1:
                                    rv = hq[:, kc * 1024 + p * 512:
                                            kc * 1024 + (p + 1) * 512]
                                else:
                                    rv = BUF_A[:, kc * 2048 + mg * 1024 + p * 512:
                                               kc * 2048 + mg * 1024 + (p + 1) * 512]
                                lv = wub[:, kc * 512 + m: (kc + 1) * 512: 4]
                                op("tensor", lambda e, ubk=ubk, lv=lv, rv=rv, kc=kc:
                                   e.matmul(ubk[:, :], lv, rv, start=(kc == 0),
                                            stop=(kc == 15)),
                                   [("spe", 1)] if kc == 15 else ())
                            st["wu_used"] = CNT["spe"]
                            umark = CNT["spe"]
                            wait("scalar", "spe", gmark)
                            wait("scalar", "sdv", st["bk_u"][mp])
                            hgb = hg_[mp]
                            op("scalar", lambda e, gbk=gbk, hgb=hgb: e.activation(
                                out=hgb, in_=gbk[:, :], func=AF.Silu), [("sac", 1)])
                            st["bk_g"][mp] = CNT["sac"]
                            hcol = hq_base + (g * 4 + m) * 1024 + p * 512
                            wait("vector", "sac", st["bk_g"][mp])
                            wait("vector", "spe", umark)
                            op("vector", lambda e, ubk=ubk, hcol=hcol, hgb=hgb, hq=hq:
                               e.tensor_mul(out=hq[:, hcol:hcol + 512], in0=hgb,
                                            in1=ubk[:, :]), [("sdv", 1)])
                            st["bk_u"][mp] = CNT["sdv"]
                        if NOSPARS:
                            continue
                        hcs = [hq_base + (g * 4 + mm) * 1024 + p * 512 for mm in range(4)]
                        wait("scalar", "sdv", CNT["sdv"])
                        for mm in range(4):
                            op("scalar", lambda e, hc=hcs[mm], mm=mm, hq=hq, spa_=spa_:
                               e.activation(out=spa_[mm], in_=hq[:, hc:hc + 512],
                                            func=AF.Abs), [("sac", 1)])
                        wait("vector", "sac")
                        prs = [(0, 1), (0, 2), (0, 3), (1, 2), (1, 3), (2, 3)]
                        for ci, (a, b) in enumerate(prs):
                            op("vector", lambda e, ci=ci, a=a, b=b, spa_=spa_, spc_=spc_:
                               e.tensor_tensor(out=spc_[ci], in0=spa_[a], in1=spa_[b],
                                               op=ALU.is_ge), [("sdv", 1)])
                        combos = [((0, 1, ALU.add), (2, ALU.add), ALU.is_ge, 1.5),
                                  ((3, 4, ALU.add), (0, ALU.subtract), ALU.is_ge, 0.5),
                                  ((1, 3, ALU.add), (5, ALU.subtract), ALU.is_le, 0.5),
                                  ((2, 4, ALU.add), (5, ALU.add), ALU.is_le, 1.5)]
                        for mm, ((ca, cb, o1), (cc, o2), kop, kth) in enumerate(combos):
                            op("vector", lambda e, mm=mm, ca=ca, cb=cb, o1=o1, spa_=spa_,
                               spc_=spc_: e.tensor_tensor(out=spa_[mm], in0=spc_[ca],
                                                          in1=spc_[cb], op=o1),
                               [("sdv", 1)])
                            op("vector", lambda e, mm=mm, cc=cc, o2=o2, spa_=spa_,
                               spc_=spc_: e.tensor_tensor(out=spa_[mm], in0=spa_[mm],
                                                          in1=spc_[cc], op=o2),
                               [("sdv", 1)])
                            op("vector", lambda e, mm=mm, kop=kop, kth=kth, spa_=spa_:
                               e.tensor_single_scalar(out=spa_[mm], in_=spa_[mm],
                                                      scalar=kth, op=kop), [("sdv", 1)])
                            op("vector", lambda e, mm=mm, hc=hcs[mm], hq=hq, spa_=spa_:
                               e.tensor_tensor(out=hq[:, hc:hc + 512],
                                               in0=hq[:, hc:hc + 512], in1=spa_[mm],
                                               op=ALU.mult), [("sdv", 1)])
                st["spars"] = CNT["sdv"]
                for half in range(nhalf):
                    if half >= len(wd_lds):
                        wd_lds.append(load_wd(Qr * nhalf + half, use_scr))
                    wd_ld = wd_lds[half]
                    wait("tensor", wd_ld[0], wd_ld[1])
                    wait("tensor", "sdv", st["spars"])
                    for tt in range(8):
                        tglob = mg * 8 + tt
                        for es in range(4):
                            par = (tt * 4 + es) % 2
                            bk = PF[par]
                            wait("tensor", "sdv", st["wd_evac"][par])
                            glist = list(range(nspan))
                            nmm = nspan * 4
                            ci = 0
                            for g2 in glist:
                                gg = half * nspan + g2
                                for m in range(4):
                                    hcol = hq_base + (gg * 4 + m) * 1024 + tt * 128
                                    wcol = (g2 * 4 + m) * 2048 + es * 512
                                    ci += 1
                                    op("tensor", lambda e, bk=bk, hcol=hcol, wcol=wcol,
                                       s0=(ci == 1), s1=(ci == nmm), hq=hq, wdb=wdb:
                                       e.matmul(bk[:, :], hq[:, hcol:hcol + 128],
                                                wdb[:, wcol:wcol + 512], start=s0,
                                                stop=s1),
                                       [("spe", 1)] if ci == nmm else ())
                            st["wd_used"] = CNT["spe"]
                            wmark = CNT["spe"]
                            wait("vector", "spe", wmark)
                            if es == 0:
                                wait("vector", "sacc",
                                     ffst["acc_busy"].get(phase, 0))
                            op("vector", lambda e, bk=bk, es=es, tac_=tac_:
                               e.tensor_scalar_mul(
                                   out=tac_[:, es * 512:(es + 1) * 512],
                                   in0=bk[:, :], scalar1=0.5), [("sdv", 1)])
                            st["wd_evac"][par] = CNT["sdv"]
                        if True:
                            wait("gpsimd", "sdv", st["wd_evac"][(tt * 4 + 3) % 2])
                            wait("gpsimd", "sacc",
                                 ffst["acc_tile"].get((phase, tglob), 0))
                            op("gpsimd", lambda e, tglob=tglob, tac_=tac_,
                               out_tiles=out_tiles: e.dma_start(
                                   out=out_tiles[tglob], in_=tac_[:, :],
                                   accum_op=ALU.add), [("sacc", 16)])
                            ffst["acc_tile"][(phase, tglob)] = CNT["sacc"]
                            ffst["acc_busy"][phase] = CNT["sacc"]

    if STAGE == "p0":
        for t in range(16):
            wait("sync", "sdv", P0_DV)
            op("sync", lambda e, t=t: e.dma_start(out=xt0, in_=x_t[t]), [("sx", 16)])
            wait("sync", "sx")
            op("sync", lambda e, t=t: e.dma_start(out=y_t[t], in_=xt0), [("sy", 16)])
        wait("sync", "sy", CNT["sy"])
        wait("tensor", "spe", P0_PE)
        return nc, Q

    ffn_pass(1)
    barrier()

    if STAGE == "p1":
        wait("sync", "sacc", CNT["sacc"])
        for t in range(16):
            wait("sync", "sy", max(0, CNT["sy"] - 16))
            op("sync", lambda e, t=t: e.dma_start(out=xt0, in_=x1_t[t]), [("sx", 16)])
            wait("sync", "sx")
            op("sync", lambda e, t=t: e.dma_start(out=y_t[t], in_=xt0), [("sy", 16)])
        wait("sync", "sy", CNT["sy"])
        return nc, Q

    for t in ([] if os.environ.get("MK_LN2", "1") == "0" else range(16)):
        if lnst["ln"]:
            wait("sync", "sdv", lnst["ln"][-1])
        op("sync", lambda e, t=t: e.dma_start(out=x1t2, in_=x1_t[t]), [("sx", 16)])
        ln_tile(x1t2, lambda kc, t=t: BUF_A[:, kc * 2048 + t * 128:
                                           kc * 2048 + (t + 1) * 128],
                xln2sc, ready=("sx", CNT["sx"]))
    LN2_DONE = CNT["sac"]

    wprep = {}

    def prep_head_weights(hh):
        sl = hh % 2
        srcs = (wq_d, wk_d, wv_d)
        qeng = ("gpsimd", "gpsimd", "gpsimd")
        qsem = ("sw", "sw", "sw")
        for wi in range(3):
            src = srcs[wi].rearrange("(h p) k -> h p k", p=128)[hh]
            stg = wstg[wi % 2]
            eng, sm = qeng[wi], qsem[wi]
            wait(eng, "spe", wprep.get(("stg", wi % 2), 0))
            wait(eng, "spe", wprep.get(("slot", sl), 0))
            op(eng, lambda e, stg=stg, src=src: e.dma_start(out=stg, in_=src),
               [(sm, 16)])
            ld = CNT[sm]
            for kc in range(16):
                wait("tensor", sm, ld)
                tpx(stg[:, kc * 128:(kc + 1) * 128],
                    wT[sl * 3 + wi][:, kc * 128:(kc + 1) * 128], "scalar")
                wprep[("stg", wi % 2)] = CNT["spe"]
        wprep[("ready", hh)] = CNT["sac"]
        msrc = mem_d.rearrange("m h d e -> h d m e")[hh]
        wait("gpsimd", "spe", wprep.get(("memfree", hh % 2), 0))
        op("gpsimd", lambda e, msrc=msrc, sl=sl: e.dma_start(
            out=memh[sl][:, :].rearrange("p (m e) -> p m e", m=4), in_=msrc),
           [("sw", 16)])
        wprep[("memld", hh)] = CNT["sw"]

    if NHEADS > 0:
        prep_head_weights(0)

    pst = {"bk01": [0, 0], "bk01s": ["sdv", "sdv"], "bk23": [0, 0],
           "bk45": [0, 0], "pb1": [0, 0], "outT": [0, 0], "exS": 0}

    for hh in range(NHEADS):
        sl = hh % 2
        wqT, wkT, wvT = wT[sl * 3 + 0], wT[sl * 3 + 1], wT[sl * 3 + 2]
        WREADY = wprep[("ready", hh)]
        MEMLD = wprep[("memld", hh)]

        wait("vector", "sexw", pst["exS"])
        op("vector", lambda e: e.memset(contrib[:, 0:136], 0.0), [("sdv", 1)])
        ZDONE = CNT["sdv"]

        def proj(wTt, out_evac):
            for p in range(4):
                pp = p % 2
                bk = PF[pp]
                wait("tensor", "sac", WREADY)
                wait("tensor", pst["bk01s"][pp], pst["bk01"][pp])
                for kc in range(16):
                    op("tensor", lambda e, bk=bk, wTt=wTt, kc=kc, p=p: e.matmul(
                        bk[:, :], wTt[:, kc * 128:(kc + 1) * 128],
                        BUF_A[:, kc * 2048 + p * 512: kc * 2048 + (p + 1) * 512],
                        start=(kc == 0), stop=(kc == 15)),
                       [("spe", 1)] if kc == 15 else ())
                out_evac(p, bk, pp)

        def k_evac(p, bk, pp):
            mk = CNT["spe"]
            wait("vector", "spe", mk)
            op("vector", lambda e, bk=bk, pp=pp: e.tensor_single_scalar(
                out=p2t[pp], in_=bk[:, :], scalar=0.0, op=ALU.min), [("sdv", 1)])
            wait("scalar", "sdv")
            op("scalar", lambda e, pp=pp: e.activation(out=p2t[pp], in_=p2t[pp],
                                                       func=AF.Exp), [("sac", 1)])
            wait("vector", "sac")
            op("vector", lambda e, bk=bk, pp=pp, p=p: e.scalar_tensor_tensor(
                out=sk_bf[:, p * 512:(p + 1) * 512], in0=bk[:, :], scalar=0.0,
                in1=p2t[pp], op0=ALU.max, op1=ALU.add), [("sdv", 1)])
            pst["bk01s"][pp], pst["bk01"][pp] = "sdv", CNT["sdv"]
        proj(wkT, k_evac)

        def v_evac(p, bk, pp):
            mk = CNT["spe"]
            wait("scalar", "spe", mk)
            op("scalar", lambda e, bk=bk, p=p: e.copy(
                out=v_bf[:, p * 512:(p + 1) * 512], in_=bk[:, :]), [("sac", 1)])
            pst["bk01s"][pp], pst["bk01"][pp] = "sac", CNT["sac"]
        proj(wvT, v_evac)
        KV_DONE = CNT["sac"]

        SKV = CNT["sdv"]
        for c in range(NCH):
            par = c % 2
            wait("tensor", "sdv", SKV)
            wait("tensor", "sac", KV_DONE)
            tpx(sk_bf[:, c * 128:(c + 1) * 128], kpT[:, c * 128:(c + 1) * 128], "vector")
            tpx(v_bf[:, c * 128:(c + 1) * 128], vT[:, c * 128:(c + 1) * 128], "vector")
            pst["pb1"][par] = CNT["sdv"]
            wait("vector", "sdv", ZDONE if c == 0 else CNT["sdv"])
            op("vector", lambda e, c=c: e.tensor_copy(
                out=spv[:, c * 132: c * 132 + 129], in_=contrib[:, 0:129]), [("sdv", 1)])
            SNAP = CNT["sdv"]
            bkv = PF[par]
            wait("tensor", pst["bk01s"][par], pst["bk01"][par])
            wait("tensor", "sdv", pst["pb1"][par])
            op("tensor", lambda e, bkv=bkv, c=c: e.matmul(
                bkv[:, 0:128], kpT[:, c * 128:(c + 1) * 128],
                vT[:, c * 128:(c + 1) * 128], start=True, stop=True), [("spe", 1)])
            op("tensor", lambda e, bkv=bkv, c=c: e.matmul(
                bkv[:, 128:129], kpT[:, c * 128:(c + 1) * 128], onescol,
                start=True, stop=True), [("spe", 1)])
            KVM = CNT["spe"]
            wait("vector", "spe", KVM)
            wait("vector", "sdv", SNAP)
            op("vector", lambda e, bkv=bkv: e.tensor_add(out=contrib[:, 0:129],
                                                         in0=contrib[:, 0:129],
                                                         in1=bkv[:, 0:129]), [("sdv", 1)])
            pst["bk01s"][par], pst["bk01"][par] = "sdv", CNT["sdv"]
        SCAN_DONE = CNT["sdv"]

        wait("gpsimd", "sdv", SCAN_DONE)

        def ex_write_S(e, hh=hh):
            with e.If_eq(regs["pvr"], 0):
                e.dma_start(out=exd[hh, :, 0:130],
                            in_=contrib[:, 0:130]).then_inc(SEM["sexw"], 16)
            with e.Else():
                e.dma_start(out=exd[H + hh, :, 0:130],
                            in_=contrib[:, 0:130]).then_inc(SEM["sexw"], 16)
        if USE_EX:
            em("gpsimd", ex_write_S)
            CNT["sexw"] += 16
        pst["exS"] = CNT["sexw"]

        def q_evac(p, bk, pp):
            mk = CNT["spe"]
            wait("scalar", "spe", mk)
            op("scalar", lambda e, bk=bk, pp=pp, p=p: e.activation(
                out=p2t[pp], in_=bk[:, :], func=AF.Identity,
                accum_out=qsumt[:, p:p + 1]), [("sac", 1)])
            QS = CNT["sac"]
            wait("vector", "sac", QS)
            op("vector", lambda e, bk=bk, pp=pp: e.tensor_single_scalar(
                out=p2t[pp], in_=bk[:, :], scalar=0.0, op=ALU.min), [("sdv", 1)])
            wait("scalar", "sdv")
            op("scalar", lambda e, pp=pp: e.activation(out=p2t[pp], in_=p2t[pp],
                                                       func=AF.Exp), [("sac", 1)])
            wait("vector", "sac")
            op("vector", lambda e, bk=bk, pp=pp, p=p: e.scalar_tensor_tensor(
                out=sq_bf[:, p * 512:(p + 1) * 512], in0=bk[:, :], scalar=0.0,
                in1=p2t[pp], op0=ALU.max, op1=ALU.add), [("sdv", 1)])
            pst["bk01s"][pp], pst["bk01"][pp] = "sdv", CNT["sdv"]
        proj(wqT, q_evac)
        SQ_DONE = CNT["sdv"]

        wait("vector", "sdv", SQ_DONE)
        op("vector", lambda e: e.reduce_sum(out=stat1, in_=qsumt, axis=AX.X), [("sdv", 1)])
        wait("vector", "sdv")
        op("vector", lambda e: e.tensor_copy(out=qsum_bf, in_=stat1), [("sdv", 1)])
        QSB = CNT["sdv"]
        wait("tensor", "sdv", QSB)
        wait("tensor", "sdv", pst["bk23"][0])
        op("tensor", lambda e, hh=hh: e.matmul(PF[2][0:1, 0:4], qsum_bf,
                                               mnT[:, hh:64:16], start=True, stop=True),
           [("spe", 1)])
        wait("vector", "spe")
        op("vector", lambda e: e.tensor_copy(out=contrib[0:1, 130:134],
                                             in_=PF[2][0:1, 0:4]), [("sdv", 1)])
        RELC = CNT["sdv"]

        wait("gpsimd", "sdv", RELC)
        wait("gpsimd", "sexw", CNT["sexw"])

        def ex_write_rel(e, hh=hh):
            with e.If_eq(regs["pvr"], 0):
                e.dma_start(out=exd[hh, 0:1, 130:134],
                            in_=contrib[0:1, 130:134]).then_inc(SEM["sexw"], 16)
            with e.Else():
                e.dma_start(out=exd[H + hh, 0:1, 130:134],
                            in_=contrib[0:1, 130:134]).then_inc(SEM["sexw"], 16)
        if USE_EX:
            em("gpsimd", ex_write_rel)
            CNT["sexw"] += 16
        wait("gpsimd", "sexw", CNT["sexw"])

        def ex_write_flag(e, hh=hh):
            with e.If_eq(regs["pvr"], 0):
                e.dma_start(out=exf[0:1, hh:hh + 1],
                            in_=flagt[0:1, hh:hh + 1]).then_inc(SEM["sexw"], 16)
            with e.Else():
                e.dma_start(out=exf[1:2, hh:hh + 1],
                            in_=flagt[0:1, hh:hh + 1]).then_inc(SEM["sexw"], 16)
        if USE_EX:
            em("gpsimd", ex_write_flag)
            CNT["sexw"] += 16

        wait("tensor", "sdv", RELC)
        for c in range(NCH):
            par = c % 2
            bsc = PF[2 + par]
            bnm = PF[4 + par]
            wait("tensor", "sdv", pst["bk23"][par])
            op("tensor", lambda e, bsc=bsc, c=c: e.matmul(
                bsc[:, 0:128], sk_bf[:, c * 128:(c + 1) * 128],
                sq_bf[:, c * 128:(c + 1) * 128], start=True, stop=True), [("spe", 1)])
            SCM = CNT["spe"]
            wait("vector", "spe", SCM)
            op("vector", lambda e, bsc=bsc, par=par: e.tensor_mul(
                out=scT[par], in0=bsc[:, 0:128], in1=trimask), [("sdv", 1)])
            pst["bk23"][par] = CNT["sdv"]
            SCS = CNT["sdv"]
            wait("tensor", "sdv", pst["bk45"][par])
            wait("tensor", "sdv", SCS)
            op("tensor", lambda e, bnm=bnm, par=par, c=c: e.matmul(
                bnm[:, 0:128], scT[par], vT[:, c * 128:(c + 1) * 128],
                start=True, stop=False), [("spe", 1)])
            op("tensor", lambda e, bnm=bnm, c=c: e.matmul(
                bnm[:, 0:128], sq_bf[:, c * 128:(c + 1) * 128],
                spv[:, c * 132: c * 132 + 128], start=False, stop=True), [("spe", 1)])
            op("tensor", lambda e, bnm=bnm, par=par: e.matmul(
                bnm[:, 128:129], scT[par], onescol, start=True, stop=False), [("spe", 1)])
            op("tensor", lambda e, bnm=bnm, c=c: e.matmul(
                bnm[:, 128:129], sq_bf[:, c * 128:(c + 1) * 128],
                spv[:, c * 132 + 128: c * 132 + 129], start=False, stop=True),
               [("spe", 1)])
            NMM = CNT["spe"]
            wait("vector", "spe", NMM)
            op("vector", lambda e, bnm=bnm, c=c: e.tensor_copy(
                out=numb[:, c * 128:(c + 1) * 128], in_=bnm[:, 0:128]), [("sdv", 1)])
            op("vector", lambda e, bnm=bnm, c=c: e.tensor_copy(
                out=denA[:, c:c + 1], in_=bnm[:, 128:129]), [("sdv", 1)])
            pst["bk45"][par] = CNT["sdv"]

        def ex_poll_read(e, hh=hh):
            fl = regs["fl"]

            def poll_body(src_flag, src_data):
                e.reg_mov(fl, 1)
                with e.While(fl):
                    e.dma_start(out=flagin[0:1, 0:1],
                                in_=src_flag).then_inc(SEM["spoll"], 16)
                    e.nop(cycle_cnt=2000)
                    e.tensor_tensor(out=flagin[0:1, 1:2], in0=flagin[0:1, 0:1],
                                    in1=flagt[0:1, hh:hh + 1], op=ALU.subtract)
                    e.nop(cycle_cnt=400)
                    e.reg_load(fl, flagin[0:1, 1:2].bitcast(I32))
                e.dma_start(out=exin[:, 0:136], in_=src_data).then_inc(
                    SEM["sexr"], 16)

            with e.If_eq(regs["pvr"], 0):
                poll_body(exf[1:2, hh:hh + 1], exd[H + hh, :, :])
            with e.Else():
                poll_body(exf[0:1, hh:hh + 1], exd[hh, :, :])
        if USE_EX:
            em("gpsimd", ex_poll_read)
            CNT["sexr"] += 16
        else:
            op("gpsimd", lambda e: e.dma_start(out=exin[:, 0:136],
                                               in_=exd[hh, :, :]), [("sexr", 16)])
        EXR = CNT["sexr"]

        wait("vector", "sexr", EXR)
        op("vector", lambda e: e.tensor_scalar_mul(out=srem[:, 0:129], in0=exin[:, 0:129],
                                                   scalar1=usest[:, 0:1]), [("sdv", 1)])
        SREMD = CNT["sdv"]
        op("vector", lambda e: e.tensor_add(out=reltot, in0=contrib[0:1, 130:134],
                                            in1=exin[0:1, 130:134]), [("sdv", 1)])
        wait("vector", "sdv")
        op("vector", lambda e: e.scalar_tensor_tensor(out=relsc[0:1, 0:4], in0=reltot,
                                                      scalar=SCALE / 4096.0, in1=bneg,
                                                      op0=ALU.mult, op1=ALU.add),
           [("sdv", 1)])
        wait("vector", "sdv")
        op("vector", lambda e: e.reduce_max(out=relsc[0:1, 4:5], in_=relsc[0:1, 0:4],
                                            axis=AX.X), [("sdv", 1)])
        wait("vector", "sdv")
        op("vector", lambda e: e.tensor_scalar_mul(out=relsc[0:1, 5:6],
                                                   in0=relsc[0:1, 4:5], scalar1=-1.0),
           [("sdv", 1)])
        wait("scalar", "sdv")
        op("scalar", lambda e: e.activation(out=wfin[0:1, 0:4], in_=relsc[0:1, 0:4],
                                            func=AF.Exp, bias=relsc[0:1, 5:6],
                                            accum_out=relsc[0:1, 6:7]), [("sac", 1)])
        wait("vector", "sac")
        op("vector", lambda e: e.reciprocal(out=relsc[0:1, 7:8], in_=relsc[0:1, 6:7]),
           [("sdv", 1)])
        wait("vector", "sdv")
        op("vector", lambda e: e.tensor_scalar_mul(out=wfin[0:1, 0:4], in0=wfin[0:1, 0:4],
                                                   scalar1=relsc[0:1, 7:8]), [("sdv", 1)])
        wait("vector", "sdv")
        op("vector", lambda e: e.tensor_mul(out=wfin[0:1, 0:4], in0=wfin[0:1, 0:4],
                                            in1=em1), [("sdv", 1)])
        wait("vector", "sdv")
        op("vector", lambda e, hh=hh: e.tensor_scalar_mul(
            out=wfin[0:1, 0:4], in0=wfin[0:1, 0:4], scalar1=sigg[0:1, hh:hh + 1]),
           [("sdv", 1)])
        wait("vector", "sdv")
        op("vector", lambda e, hh=hh: e.tensor_copy(out=wfin[0:1, 4:5],
                                                    in_=omg[0:1, hh:hh + 1]), [("sdv", 1)])
        wait("vector", "sdv")
        op("vector", lambda e: e.tensor_copy(out=wfin_bf[0:1, 0:5], in_=wfin[0:1, 0:5]),
           [("sdv", 1)])
        WFB = CNT["sdv"]
        wait("tensor", "sdv", WFB)
        wait("tensor", "sdv", pst["bk23"][0])
        op("tensor", lambda e: e.matmul(PF[2][:, 384:389], onesrow, wfin_bf[0:1, 0:5],
                                        start=True, stop=True), [("spe", 1)])
        wait("vector", "spe")
        op("vector", lambda e: e.tensor_copy(out=wrep[:, 0:5], in_=PF[2][:, 384:389]),
           [("sdv", 1)])
        WREP = CNT["sdv"]
        pst["bk23"][0] = WREP

        for c in range(NCH):
            par = c % 2
            brm = PF[2 + par]
            wait("tensor", "sdv", pst["bk23"][par])
            wait("tensor", "sdv", SREMD)
            op("tensor", lambda e, brm=brm, c=c: e.matmul(
                brm[:, 0:128], sq_bf[:, c * 128:(c + 1) * 128], srem[:, 0:128],
                start=True, stop=True), [("spe", 1)])
            op("tensor", lambda e, brm=brm, c=c: e.matmul(
                brm[:, 128:129], sq_bf[:, c * 128:(c + 1) * 128], srem[:, 128:129],
                start=True, stop=True), [("spe", 1)])
            RMM = CNT["spe"]
            wait("vector", "spe", RMM)
            op("vector", lambda e, brm=brm, c=c: e.tensor_add(
                out=numb[:, c * 128:(c + 1) * 128], in0=numb[:, c * 128:(c + 1) * 128],
                in1=brm[:, 0:128]), [("sdv", 1)])
            op("vector", lambda e, brm=brm, c=c: e.tensor_copy(
                out=denR[:, c:c + 1], in_=brm[:, 128:129]), [("sdv", 1)])
            pst["bk23"][par] = CNT["sdv"]
        B1_DONE = CNT["sdv"]

        wait("vector", "sdv", B1_DONE)
        op("vector", lambda e: e.tensor_add(out=rdent, in0=denA, in1=denR), [("sdv", 1)])
        wait("vector", "sdv")
        op("vector", lambda e: e.tensor_single_scalar(out=rdent, in_=rdent,
                                                      scalar=EPS_AT, op=ALU.max),
           [("sdv", 1)])
        wait("vector", "sdv")
        op("vector", lambda e: e.reciprocal(out=rdent, in_=rdent), [("sdv", 1)])
        wait("vector", "sdv")
        op("vector", lambda e: e.tensor_scalar_mul(out=rdent, in0=rdent,
                                                   scalar1=wrep[:, 4:5]), [("sdv", 1)])
        DEN_DONE = CNT["sdv"]

        wait("tensor", "sdv", pst["bk23"][0])
        wait("tensor", "sdv", pst["bk23"][1])
        for c in range(NCH):
            op("tensor", lambda e, c=c, hh=hh: e.matmul(
                PF[2][:, c * 4:(c + 1) * 4], sq_bf[:, c * 128:(c + 1) * 128],
                mnT[:, hh:64:16], start=True, stop=True),
               [("spe", 1)] if c == NCH - 1 else ())
        NRM = CNT["spe"]
        wait("vector", "spe", NRM)
        op("vector", lambda e: e.tensor_single_scalar(out=sfacall, in_=PF[2][:, 0:64],
                                                      scalar=EPS_AT, op=ALU.max),
           [("sdv", 1)])
        pst["bk23"][0] = CNT["sdv"]
        pst["bk23"][1] = CNT["sdv"]
        wait("vector", "sdv")
        op("vector", lambda e: e.reciprocal(out=sfacall, in_=sfacall), [("sdv", 1)])
        wait("vector", "sdv")
        wait("vector", "sdv", WREP)
        for m in range(4):
            op("vector", lambda e, m=m: e.tensor_scalar_mul(
                out=sfacall[:, m:64:4], in0=sfacall[:, m:64:4],
                scalar1=wrep[:, m:m + 1]), [("sdv", 1)])
        SFAC_DONE = CNT["sdv"]

        for c in range(NCH):
            par = c % 2
            bmm = PF[4 + par]
            wait("tensor", "sdv", pst["bk45"][par])
            wait("tensor", "sw", MEMLD)
            op("tensor", lambda e, bmm=bmm, c=c, sl=sl: e.matmul(
                bmm[:, 0:512], sq_bf[:, c * 128:(c + 1) * 128], memh[sl][:, :],
                start=True, stop=True), [("spe", 1)])
            MMM = CNT["spe"]
            wait("vector", "spe", MMM)
            wait("vector", "sdv", SFAC_DONE)
            wait("vector", "sdv", DEN_DONE)
            if NOLOC:
                op("vector", lambda e: e.memset(p2out, 0.0), [("sdv", 1)])
            else:
                op("vector", lambda e, c=c: e.tensor_scalar_mul(
                    out=p2out, in0=numb[:, c * 128:(c + 1) * 128],
                    scalar1=rdent[:, c:c + 1]), [("sdv", 1)])
            wait("vector", "sdv")
            for m in ([] if NOMEM else range(4)):
                op("vector", lambda e, bmm=bmm, m=m, c=c: e.scalar_tensor_tensor(
                    out=p2out, in0=bmm[:, m * 128:(m + 1) * 128],
                    scalar=sfacall[:, c * 4 + m: c * 4 + m + 1], in1=p2out,
                    op0=ALU.mult, op1=ALU.add), [("sdv", 1)])
                wait("vector", "sdv")
            pst["bk45"][par] = CNT["sdv"]
            op("vector", lambda e, par=par: e.tensor_copy(out=scT[par], in_=p2out),
               [("sdv", 1)])
            CST = CNT["sdv"]
            wait("tensor", "sdv", CST)
            mk = tpx(scT[par], BUF_B[:, hh * 2048 + c * 128: hh * 2048 + (c + 1) * 128],
                     "scalar")
            pst["outT"][par] = mk

        if hh + 1 < NHEADS:
            wprep[("slot", (hh + 1) % 2)] = CNT["spe"]
            wprep[("memfree", (hh + 1) % 2)] = CNT["spe"]
            prep_head_weights(hh + 1)

    if STAGE == "dbg2":
        def dump2(src_ap, t):
            wait("vector", "sdv", CNT["sdv"])
            wait("vector", "sac", CNT["sac"])
            wait("vector", "sy", CNT["sy"])
            op("vector", lambda e, src_ap=src_ap: e.tensor_copy(out=x2t, in_=src_ap),
               [("sdv", 1)])
            wait("sync", "sdv", CNT["sdv"])
            op("sync", lambda e, t=t: e.dma_start(out=y_t[t], in_=x2t), [("sy", 16)])
        for t in range(16):
            dump2(BUF_B[:, t * 2048:(t + 1) * 2048], t)
        wait("sync", "sy", CNT["sy"])
        return nc, Q

    if STAGE == "dbg":
        def dump_bf(src_ap, t):
            wait("vector", "sdv", CNT["sdv"])
            wait("vector", "sac", CNT["sac"])
            wait("vector", "sy", CNT["sy"])
            op("vector", lambda e, src_ap=src_ap: e.tensor_copy(out=x2t, in_=src_ap),
               [("sdv", 1)])
            wait("sync", "sdv", CNT["sdv"])
            op("sync", lambda e, t=t: e.dma_start(out=y_t[t], in_=x2t), [("sy", 16)])
            wait("sync", "sy", CNT["sy"])

        dump_bf(numb[:, :], 0)
        wait("vector", "sy", CNT["sy"])
        op("vector", lambda e: e.memset(x2t, 0.0), [("sdv", 1)])
        wait("vector", "sdv")
        op("vector", lambda e: e.tensor_copy(out=x2t[:, 0:16], in_=denA), [("sdv", 1)])
        op("vector", lambda e: e.tensor_copy(out=x2t[:, 16:32], in_=denR), [("sdv", 1)])
        op("vector", lambda e: e.tensor_copy(out=x2t[:, 32:48], in_=rdent), [("sdv", 1)])
        op("vector", lambda e: e.tensor_copy(out=x2t[:, 48:180], in_=srem[:, 0:132]),
           [("sdv", 1)])
        op("vector", lambda e: e.tensor_copy(out=x2t[:, 180:316], in_=contrib),
           [("sdv", 1)])
        wait("sync", "sdv", CNT["sdv"])
        op("sync", lambda e: e.dma_start(out=y_t[1], in_=x2t), [("sy", 16)])
        dump_bf(sq_bf[:, :], 2)
        dump_bf(sk_bf[:, :], 3)
        dump_bf(v_bf[:, :], 4)
        dump_bf(spv[:, 0:2048], 5)
        dump_bf(kpT[:, :], 6)
        dump_bf(vT[:, :], 7)
        DH = int(os.environ.get("MK_DH", "0"))
        dump_bf(BUF_B[:, DH * 2048:(DH + 1) * 2048], 8)
        dump_bf(BUF_A[:, 0:2048], 9)
        wait("sync", "sy", CNT["sy"])
        return nc, Q

    barrier()

    if STAGE == "p2":
        wait("sync", "sacc", CNT["sacc"])
        for t in range(16):
            op("sync", lambda e, t=t: e.dma_start(out=x2t, in_=x1_t[t]), [("sx", 16)])
            wait("sync", "sx")
            op("sync", lambda e, t=t: e.dma_start(out=y_t[t], in_=x2t), [("sy", 16)])
        wait("sync", "sy", CNT["sy"])
        return nc, Q

    wo_r = wo_d.rearrange("(q ec p) (h dd) -> h q p ec dd", q=4, ec=4, p=128,
                          dd=128)
    wst3 = {"stg": 0, "tp": [0, 0]}
    for h2 in range(H):
        wait("gpsimd", "spe", wst3["stg"])
        wo_q = []
        for qq in range(4):
            op("gpsimd", lambda e, h2=h2, qq=qq: e.dma_start(
                out=wostg[:, qq * 512:(qq + 1) * 512]
                .rearrange("p (ec dd) -> p ec dd", ec=4), in_=wo_r[h2, qq]),
               [("sw", 16)])
            wo_q.append(CNT["sw"])
        for ec in range(16):
            wait("tensor", "sw", wo_q[ec // 4])
            tpx(wostg[:, ec * 128:(ec + 1) * 128],
                woT[:, h2 * 2048 + ec * 128: h2 * 2048 + (ec + 1) * 128], "scalar")
            wst3["stg"] = CNT["spe"]
    WOT_DONE = CNT["sac"]

    if STAGE == "dbgwo":
        def dump3(src_ap, t):
            wait("vector", "sdv", CNT["sdv"])
            wait("vector", "sac", CNT["sac"])
            wait("vector", "sy", CNT["sy"])
            op("vector", lambda e, src_ap=src_ap: e.tensor_copy(out=x2t, in_=src_ap),
               [("sdv", 1)])
            wait("sync", "sdv", CNT["sdv"])
            op("sync", lambda e, t=t: e.dma_start(out=y_t[t], in_=x2t), [("sy", 16)])
        for t in range(16):
            dump3(woT[:, t * 2048:(t + 1) * 2048], t)
        wait("sync", "sy", CNT["sy"])
        return nc, Q

    p3 = {"evac": [0, 0], "x2free": 0}
    for t in range(16):
        if lnst["ln"]:
            wait("sync", "sdv", lnst["ln"][-1])
        wait("sync", "sy", p3["x2free"])
        op("sync", lambda e, t=t: e.dma_start(out=x2t, in_=x1_t[t]), [("sx", 16)])
        XLD = CNT["sx"]
        for es in range(4):
            par = es % 2
            bk = PF[par]
            wait("tensor", "sdv", p3["evac"][par])
            wait("tensor", "sac", WOT_DONE)
            for h2 in range(16):
                op("tensor", lambda e, bk=bk, h2=h2, t=t, es=es: e.matmul(
                    bk[:, :], BUF_B[:, h2 * 2048 + t * 128: h2 * 2048 + (t + 1) * 128],
                    woT[:, h2 * 2048 + es * 512: h2 * 2048 + (es + 1) * 512],
                    start=(h2 == 0), stop=(h2 == 15)),
                   [("spe", 1)] if h2 == 15 else ())
            PM = CNT["spe"]
            wait("vector", "spe", PM)
            wait("vector", "sx", XLD)
            op("vector", lambda e, bk=bk, es=es: e.tensor_add(
                out=x2t[:, es * 512:(es + 1) * 512],
                in0=x2t[:, es * 512:(es + 1) * 512], in1=bk[:, :]), [("sdv", 1)])
            p3["evac"][par] = CNT["sdv"]
        X2DONE = CNT["sdv"]
        wait("sync", "sdv", X2DONE)
        op("sync", lambda e, t=t: e.dma_start(out=y_t[t], in_=x2t), [("sy", 16)])
        p3["x2free"] = CNT["sy"]
        ln_tile(x2t, lambda kc, t=t: BUF_A[:, kc * 2048 + t * 128:
                                          kc * 2048 + (t + 1) * 128],
                xln3sc, ready=("sdv", X2DONE))

    barrier()
    wait("gpsimd", "sy", CNT["sy"])

    if STAGE == "p3":
        wait("sync", "sy", CNT["sy"])
        return nc, Q

    ffn_pass(4)

    wait("sync", "sacc", CNT["sacc"])
    wait("sync", "sy", CNT["sy"])
    return nc, Q


def _assemble(nc, Q):
    with nc.Block() as block:
        block.sync(lambda e: [f(e) for f in Q["sync"]])
        block.tensor(lambda e: [f(e) for f in Q["tensor"]])
        block.vector(lambda e: [f(e) for f in Q["vector"]])
        block.scalar(lambda e: [f(e) for f in Q["scalar"]])
        block.gpsimd(lambda e: [f(e) for f in Q["gpsimd"]])
    return nc


_CACHE = {}


def _get_nc():
    if STAGE not in _CACHE:
        nc, Q = build()
        _CACHE[STAGE] = _assemble(nc, Q)
    return _CACHE[STAGE]


def kernel(L0_attn_hidden_states, L0_attn_memories, L0_attn_memory_norms,
           L0_attn_w_q, L0_attn_w_k, L0_attn_w_v, L0_attn_w_o, L0_attn_gate,
           L0_ffn_w_gate, L0_ffn_w_up, L0_ffn_w_down):
    import ml_dtypes
    nc = _get_nc()
    f32 = np.float32
    x = np.ascontiguousarray(np.asarray(L0_attn_hidden_states, dtype=f32))
    args = {
        "wq": np.ascontiguousarray(np.asarray(L0_attn_w_q, dtype=f32)),
        "wk": np.ascontiguousarray(np.asarray(L0_attn_w_k, dtype=f32)),
        "wv": np.ascontiguousarray(np.asarray(L0_attn_w_v, dtype=f32)),
        "wo": np.ascontiguousarray(np.asarray(L0_attn_w_o, dtype=f32)),
        "wg_in": np.ascontiguousarray(np.asarray(L0_ffn_w_gate, dtype=f32)),
        "wu_in": np.ascontiguousarray(np.asarray(L0_ffn_w_up, dtype=f32)),
        "wd_in": np.ascontiguousarray(np.asarray(L0_ffn_w_down, dtype=f32)),
        "mem_in": np.ascontiguousarray(np.asarray(L0_attn_memories, dtype=f32)),
        "mn_in": np.ascontiguousarray(np.asarray(L0_attn_memory_norms, dtype=f32)),
        "gate_in": np.ascontiguousarray(
            np.asarray(L0_attn_gate, dtype=f32)).reshape(1, H),
        "trimask_in": np.triu(np.ones((128, 128))).astype(ml_dtypes.bfloat16),
        "identb_in": np.eye(128).astype(ml_dtypes.bfloat16),
        "onescol_in": np.ones((128, 1), dtype=ml_dtypes.bfloat16),
        "onesrow_in": np.ones((1, 128), dtype=ml_dtypes.bfloat16),
    }
    nonce = int(np.random.randint(1, 2 ** 20))
    flagv = np.zeros((1, 24), dtype=np.float32)
    flagv[0, :16] = (nonce + 1 + np.arange(16)).astype(np.float32)
    in_maps = []
    for c in range(8):
        b, half = c // 2, c % 2
        m = dict(args)
        m["x_in"] = np.ascontiguousarray(x[b, half * TOK:(half + 1) * TOK, :])
        m["pidx_in"] = np.array([[half, 1 - half]], dtype=np.int32)
        m["usest_in"] = np.full((128, 1), float(half), dtype=f32)
        m["flagv_in"] = flagv
        in_maps.append(m)
    res = run_bass_kernel_spmd(nc, in_maps, core_ids=list(range(8)))
    out = np.empty((4, 4096, D), dtype=f32)
    for c in range(8):
        b, half = c // 2, c % 2
        out[b, half * TOK:(half + 1) * TOK, :] = res.results[c]["y"]
    return out


# version 04 (final)
# speedup vs baseline: 1.2024x; 1.0004x over previous
import os
import numpy as np
import concourse.bass as bass
import concourse.mybir as mybir
from concourse.bass_utils import run_bass_kernel_spmd

F32 = mybir.dt.float32
BF16 = mybir.dt.bfloat16
I32 = mybir.dt.int32
AF = mybir.ActivationFunctionType
ALU = mybir.AluOpType
AX = mybir.AxisListType
ET = mybir.EngineType

D = 2048
TOK = 2048
H = 16
HD = 128
M = 4
NCH = 16
IF = 8192
NMEGA = 2
EPS_LN = 1e-5
EPS_AT = 1e-6
SCALE = HD ** -0.5

STAGE = os.environ.get("MK_STAGE", "full")
NQR = int(os.environ.get("MK_NQ", "4"))
LNLVL = int(os.environ.get("MK_LN", "4"))
NOSPARS = os.environ.get("MK_NOSPARS", "0") == "1"
NHEADS = int(os.environ.get("MK_NH", "16"))
USE_EX = os.environ.get("MK_EX", "1") == "1"
NOMEM = os.environ.get("MK_NOMEM", "0") == "1"
NOLOC = os.environ.get("MK_NOLOC", "0") == "1"
USE_SCR = os.environ.get("MK_SCR", "1") == "1"

OFF_A = 20448
OFF_B = 85984
DYN0 = 151520

ENGS = ("sync", "tensor", "vector", "scalar", "gpsimd")


def build():
    nc = bass.Bass("TRN2", target_bir_lowering=False, monotonic_sem_count=0)
    Q = {e: [] for e in ENGS}
    CNT = {}
    SEM = {}

    def sem(name):
        SEM[name] = nc.alloc_semaphore(name)
        CNT[name] = 0

    def em(eng, fn):
        Q[eng].append(fn)

    WMAX = {e: {} for e in ENGS}

    def wait(eng, name, val=None):
        v = CNT[name] if val is None else val
        if v <= 0 or WMAX[eng].get(name, 0) >= v:
            return
        WMAX[eng][name] = v
        em(eng, lambda e, h=SEM[name], v=v: e.wait_ge(h, v))

    def op(eng, makefn, incs=()):
        hs = []
        for s, k in incs:
            CNT[s] += k
            hs.append((SEM[s], k))

        def run(e, makefn=makefn, hs=hs):
            ins = makefn(e)
            for h, k in hs:
                ins = ins.then_inc(h, k)
        em(eng, run)

    bar_i = [0]

    def barrier():
        bar_i[0] += 1
        s = f"bar{bar_i[0]}"
        sem(s)
        for e in ENGS:
            op(e, lambda en, h=SEM[s]: en.sem_inc(h, 1))
        CNT[s] = 5
        for e in ENGS:
            wait(e, s, 5)

    dt_ = nc.dram_tensor
    x_in = dt_("x_in", [TOK, D], F32, kind="ExternalInput").ap()
    wq_d = dt_("wq", [D, D], F32, kind="ExternalInput").ap()
    wk_d = dt_("wk", [D, D], F32, kind="ExternalInput").ap()
    wv_d = dt_("wv", [D, D], F32, kind="ExternalInput").ap()
    wo_d = dt_("wo", [D, D], F32, kind="ExternalInput").ap()
    wg_d = dt_("wg_in", [D, IF], F32, kind="ExternalInput").ap()
    wu_d = dt_("wu_in", [D, IF], F32, kind="ExternalInput").ap()
    wd_d = dt_("wd_in", [IF, D], F32, kind="ExternalInput").ap()
    mem_d = dt_("mem_in", [M, H, HD, HD], F32, kind="ExternalInput").ap()
    mn_d = dt_("mn_in", [M, H, HD], F32, kind="ExternalInput").ap()
    gate_d = dt_("gate_in", [1, H], F32, kind="ExternalInput").ap()
    trim_d = dt_("trimask_in", [128, 128], BF16, kind="ExternalInput").ap()
    idb_d = dt_("identb_in", [128, 128], BF16, kind="ExternalInput").ap()
    onc_d = dt_("onescol_in", [128, 1], BF16, kind="ExternalInput").ap()
    onr_d = dt_("onesrow_in", [1, 128], BF16, kind="ExternalInput").ap()
    ust_d = dt_("usest_in", [128, 1], F32, kind="ExternalInput").ap()
    pid_d = dt_("pidx_in", [1, 2], I32, kind="ExternalInput").ap()
    flg_d = dt_("flagv_in", [1, 24], F32, kind="ExternalInput").ap()

    y = dt_("y", [TOK, D], F32, kind="ExternalOutput").ap()
    x1s = dt_("x1_scr", [TOK, D], F32, kind="Internal").ap()
    wgsc = dt_("wg_sc", [16, 128, 8192], BF16, kind="Internal").ap()
    wusc = dt_("wu_sc", [16, 128, 8192], BF16, kind="Internal").ap()
    wdsc = dt_("wd_sc", [4, 128, 32768], BF16, kind="Internal").ap()
    exd = dt_("exd", [2 * H, 128, 136], F32, kind="Internal",
              addr_space="Shared").ap()
    exf = dt_("exf", [2, 32], F32, kind="Internal", addr_space="Shared").ap()

    A_ = nc.alloc_sbuf_tensor
    trimask = A_("trimask", [128, 128], BF16).ap()
    identb = A_("identb", [128, 128], BF16).ap()
    onescol = A_("onescol", [128, 1], BF16).ap()
    onesrow = A_("onesrow", [1, 128], BF16).ap()
    usest = A_("usest", [128, 1], F32).ap()
    pidxt = A_("pidxt", [1, 2], I32).ap()
    flagt = A_("flagt", [1, 24], F32).ap()
    flagin = A_("flagin", [1, 8], F32).ap()
    gatet = A_("gatet", [1, H], F32).ap()
    sigg = A_("sigg", [1, H], F32).ap()
    omg = A_("omg", [1, H], F32).ap()
    etile = A_("etile", [1, 4], F32).ap()
    bneg = A_("bneg", [1, 4], F32).ap()
    em1 = A_("em1", [1, 4], F32).ap()
    mnT = A_("mnT", [128, 64], BF16).ap()
    contrib = A_("contrib", [128, 136], F32).ap()
    exin = A_("exin", [128, 136], F32).ap()
    srem = A_("srem", [128, 132], BF16).ap()
    stat1 = A_("stat1", [128, 1], F32).ap()
    stat2 = A_("stat2", [128, 1], F32).ap()
    stat3 = A_("stat3", [128, 1], F32).ap()
    qsumt = A_("qsumt", [128, 4], F32).ap()
    qsum_bf = A_("qsum_bf", [128, 1], BF16).ap()
    reltot = A_("reltot", [1, 4], F32).ap()
    relsc = A_("relsc", [1, 8], F32).ap()
    wfin = A_("wfin", [1, 8], F32).ap()
    wfin_bf = A_("wfin_bf", [1, 8], BF16).ap()
    wrep = A_("wrep", [128, 8], F32).ap()
    denA = A_("denA", [128, 16], F32).ap()
    denR = A_("denR", [128, 16], F32).ap()
    rdent = A_("rdent", [128, 16], F32).ap()

    _pr = A_("pr_probe", [128, 8], F32)
    _pa = nc.lookup_mloc(_pr).addr
    assert _pa + 32 <= OFF_A, f"persistent SBUF region overflows into BUF_A: {_pa}"

    def at(name, shape, dtype, off):
        assert off + int(np.prod(shape[1:])) * (2 if dtype == BF16 else 4) <= 229344, name
        return nc.alloc_sbuf_tensor_at(name, shape, dtype, offset=off).ap()

    BUF_A = at("BUF_A", [128, 16 * 2048], BF16, OFF_A)
    BUF_B = at("BUF_B", [128, 16 * 2048], BF16, OFF_B)

    wgq = [at(f"wgq{i}", [128, 16 * 512], BF16, DYN0 + 16384 * i) for i in range(2)]
    wuq = at("wuq", [128, 16 * 512], BF16, DYN0 + 32768)
    xt0 = at("xt0", [128, 2048], F32, DYN0 + 49152)
    tacc = at("tacc", [128, 2048], F32, DYN0 + 49152)
    xlnsc = at("xlnsc", [128, 2048], BF16, DYN0 + 57344)
    hgate = at("hgate", [128, 512], F32, DYN0 + 61440)
    hgateB = at("hgateB", [128, 512], F32, DYN0 + 73728)
    spa = [at(f"spa{i}", [128, 512], BF16, DYN0 + 63488 + 1024 * i) for i in range(4)]
    spc = [at(f"spc{i}", [128, 512], BF16, DYN0 + 67584 + 1024 * i) for i in range(6)]
    wdq1 = at("wdq1", [128, 16 * 2048], BF16, OFF_A)

    wT = [at(f"wT{i}", [128, 2048], BF16, DYN0 + 4096 * i) for i in range(6)]
    wstg = [at(f"wstg{i}", [128, 2048], BF16, DYN0 + 24576 + 4096 * i) for i in range(2)]
    sq_bf = at("sq_bf", [128, 2048], BF16, DYN0 + 32768)
    sk_bf = at("sk_bf", [128, 2048], BF16, DYN0 + 36864)
    v_bf = at("v_bf", [128, 2048], BF16, DYN0 + 40960)
    kpT = at("kpT", [128, 2048], BF16, DYN0 + 45056)
    vT = at("vT", [128, 2048], BF16, DYN0 + 49152)
    spv = at("spv", [128, 16 * 132], BF16, DYN0 + 53248)
    numb = at("numb", [128, 16 * 128], F32, DYN0 + 57600)
    memh = [at(f"memh{i}", [128, 512], BF16, DYN0 + 65792 + 1024 * i) for i in range(2)]
    p2t = [at(f"p2t{i}", [128, 512], F32, DYN0 + 67840 + 2048 * i) for i in range(2)]
    p2out = at("p2out", [128, 128], F32, DYN0 + 71936)
    scT = [at(f"scT{i}", [128, 128], BF16, DYN0 + 72448 + 256 * i) for i in range(2)]
    sfacall = at("sfacall", [128, 64], F32, DYN0 + 72960)
    x1t2 = at("x1t2", [128, 2048], F32, DYN0 + 32768)
    xln2sc = at("xln2sc", [128, 2048], BF16, DYN0 + 45056)

    woT = at("woT", [128, 16 * 2048], BF16, DYN0 + 0)
    wostg = at("wostg", [128, 2048], BF16, DYN0 + 65536)
    x2t = at("x2t", [128, 2048], F32, DYN0 + 69632)
    xln3sc = at("xln3sc", [128, 2048], BF16, DYN0 + 65536)

    wdh4 = at("wdh4", [128, 8 * 2048], BF16, DYN0 + 0)
    wuq4 = at("wuq4", [128, 16 * 512], BF16, DYN0 + 32768)
    hgate4 = at("hgate4", [128, 512], F32, DYN0 + 49152)
    hgate4B = at("hgate4B", [128, 512], F32, DYN0 + 69632)
    spa4 = [at(f"spa4_{i}", [128, 512], BF16, DYN0 + 51200 + 1024 * i) for i in range(4)]
    spc4 = [at(f"spc4_{i}", [128, 512], BF16, DYN0 + 55296 + 1024 * i) for i in range(6)]
    tacc4 = at("tacc4", [128, 2048], F32, DYN0 + 61440)
    wgq4 = [at(f"wgq4_{i}", [128, 16 * 512], BF16, OFF_B + 32768 + 16384 * i)
            for i in range(2)]

    PF = [nc.alloc_psum_tensor(f"PF{i}", [128, 512], F32).ap() for i in range(6)]
    PB1 = nc.alloc_psum_tensor("PB1", [128, 1024], BF16).ap()
    PB2 = nc.alloc_psum_tensor("PB2", [128, 1024], BF16).ap()

    for s in ("sx", "sw", "swu", "swd", "swb", "swbd", "sacc", "spe", "sdv",
              "sac", "sexw", "spoll", "sexr", "sy"):
        sem(s)

    x_t = x_in.rearrange("(t p) d -> t p d", p=128)
    x1_t = x1s.rearrange("(t p) d -> t p d", p=128)
    y_t = y.rearrange("(t p) d -> t p d", p=128)

    regs = {}

    TP = {"idx": 0, "busy": [None, None]}
    ESEM = {"scalar": "sac", "vector": "sdv"}

    def tpx(src_ap, dst_ap, evac_eng="scalar", ncols=128, ident=None):
        b = TP["idx"] % 2
        TP["idx"] += 1
        bank = PB1 if b == 0 else PB2
        if TP["busy"][b] is not None:
            s, v = TP["busy"][b]
            wait("tensor", s, v)
        idn = identb if ident is None else ident
        op("tensor", lambda e, bank=bank, src_ap=src_ap, idn=idn, ncols=ncols:
           e.transpose(bank[0:src_ap.shape[1], 0:ncols], src_ap, idn), [("spe", 1)])
        m = CNT["spe"]
        wait(evac_eng, "spe", m)
        sname = ESEM[evac_eng]
        if evac_eng == "scalar":
            op(evac_eng, lambda e, dst_ap=dst_ap, bank=bank, src_ap=src_ap, ncols=ncols:
               e.copy(out=dst_ap, in_=bank[0:src_ap.shape[1], 0:ncols]), [(sname, 1)])
        else:
            op(evac_eng, lambda e, dst_ap=dst_ap, bank=bank, src_ap=src_ap, ncols=ncols:
               e.tensor_copy(out=dst_ap, in_=bank[0:src_ap.shape[1], 0:ncols]),
               [(sname, 1)])
        TP["busy"][b] = (sname, CNT[sname])
        return CNT[sname]

    for dst, src in ((trimask, trim_d[:, :]), (identb, idb_d[:, :]),
                     (onescol, onc_d[:, :]), (onesrow, onr_d[:, :]),
                     (usest, ust_d[:, :]), (pidxt, pid_d[:, :]),
                     (flagt, flg_d[:, :]), (gatet, gate_d[:, :])):
        op("sync", lambda e, dst=dst, src=src: e.dma_start(out=dst, in_=src),
           [("sx", 16)])
    mn64 = mn_d.rearrange("m h d -> (m h) d")
    mn4 = mn_d.rearrange("m h d -> m (h d)")
    op("sync", lambda e: e.dma_start(out=xt0[0:64, 0:128], in_=mn64[:, :]), [("sx", 16)])
    op("sync", lambda e: e.dma_start(out=xt0[64:68, 0:2048], in_=mn4[:, :]), [("sx", 16)])
    wait("scalar", "sx")
    op("scalar", lambda e: e.activation(out=sigg, in_=gatet, func=AF.Sigmoid), [("sac", 1)])
    wait("vector", "sac")
    op("vector", lambda e: e.tensor_scalar(out=omg, in0=sigg, scalar1=-1.0, scalar2=1.0,
                                           op0=ALU.mult, op1=ALU.add), [("sdv", 1)])
    wait("vector", "sx")
    op("vector", lambda e: e.tensor_copy(out=spa[0][0:64, 0:128], in_=xt0[0:64, 0:128]),
       [("sdv", 1)])
    wait("tensor", "sdv")
    tpx(spa[0][0:64, 0:128], mnT, "vector", ncols=64, ident=identb[0:64, 0:64])
    op("vector", lambda e: e.reduce_sum(out=stat1[0:4, 0:1], in_=xt0[64:68, 0:2048],
                                        axis=AX.X), [("sdv", 1)])
    wait("vector", "sdv")
    op("vector", lambda e: e.tensor_copy(out=spa[1][0:4, 0:1], in_=stat1[0:4, 0:1]),
       [("sdv", 1)])
    wait("tensor", "sdv")
    tpx(spa[1][0:4, 0:1], relsc[0:1, 4:8], "vector", ncols=4, ident=identb[0:4, 0:4])
    wait("vector", "sdv")
    op("vector", lambda e: e.tensor_single_scalar(out=etile, in_=relsc[0:1, 4:8],
                                                  scalar=1e-6, op=ALU.is_lt), [("sdv", 1)])
    wait("vector", "sdv")
    op("vector", lambda e: e.tensor_scalar_mul(out=bneg, in0=etile, scalar1=-1e30),
       [("sdv", 1)])
    op("vector", lambda e: e.tensor_scalar(out=em1, in0=etile, scalar1=-1.0, scalar2=1.0,
                                           op0=ALU.mult, op1=ALU.add), [("sdv", 1)])
    P0_DV = CNT["sdv"]
    P0_PE = CNT["spe"]
    wait("gpsimd", "sx")

    def _loadregs(e):
        regs["pvr"] = e.alloc_register("pvr")
        regs["fl"] = e.alloc_register("flpoll")
        e.reg_load(regs["pvr"], pidxt[0:1, 0:1])
    em("gpsimd", _loadregs)

    lnst = {"ln": [], "tp": []}

    def ln_tile(xsrc, dst_col_fn, sq_scr, ready=None):
        if LNLVL < 2:
            return CNT["sac"]
        if ready is not None:
            wait("vector", ready[0], ready[1])
        op("vector", lambda e, xsrc=xsrc: e.reduce_sum(out=stat1, in_=xsrc, axis=AX.X),
           [("sdv", 1)])
        wait("vector", "sdv")
        op("vector", lambda e: e.tensor_scalar_mul(out=stat1, in0=stat1, scalar1=-1.0 / D),
           [("sdv", 1)])
        wait("scalar", "sdv")
        op("scalar", lambda e, xsrc=xsrc, sq_scr=sq_scr: e.activation(
            out=sq_scr, in_=xsrc, func=AF.Square, bias=stat1[:, 0:1], scale=1.0,
            accum_out=stat2[:, 0:1]), [("sac", 1)])
        wait("vector", "sac")
        op("vector", lambda e: e.tensor_scalar(out=stat3, in0=stat2, scalar1=1.0 / D,
                                               scalar2=EPS_LN, op0=ALU.mult, op1=ALU.add),
           [("sdv", 1)])
        wait("scalar", "sdv")
        op("scalar", lambda e: e.sqrt(out=stat3, in_=stat3), [("sac", 1)])
        wait("vector", "sac")
        op("vector", lambda e: e.reciprocal(out=stat3, in_=stat3), [("sdv", 1)])
        if LNLVL < 3:
            lnst["ln"].append(CNT["sdv"])
            return CNT["sac"]
        wait("scalar", "sdv")
        op("scalar", lambda e, xsrc=xsrc, sq_scr=sq_scr: e.activation(
            out=sq_scr, in_=xsrc, func=AF.Identity, bias=stat1[:, 0:1], scale=1.0),
           [("sac", 1)])
        wait("vector", "sac")
        op("vector", lambda e, sq_scr=sq_scr: e.tensor_scalar_mul(
            out=sq_scr, in0=sq_scr, scalar1=stat3[:, 0:1]), [("sdv", 1)])
        lnst["ln"].append(CNT["sdv"])
        if LNLVL < 4:
            return CNT["sac"]
        for kc in range(16):
            wait("tensor", "sdv", lnst["ln"][-1])
            mk = tpx(sq_scr[:, kc * 128:(kc + 1) * 128], dst_col_fn(kc), "scalar")
            lnst["tp"].append(mk)
        return CNT["sac"]

    ffst = {"acc_tile": {}, "acc_busy": {}, "wg_wb": [0, 0], "wu_wb": 0,
            "wd_wb": 0}

    def ffn_pass(phase):
        if phase == 1:
            hq, hq_base = BUF_B, 16384
            wgb, wub, wdb = wgq, wuq, wdq1
            spa_, spc_, hg_, tac_ = spa, spc, (hgate, hgateB), tacc
            out_tiles, nspan = x1_t, 4
        else:
            hq, hq_base = BUF_B, 0
            wgb, wub, wdb = wgq4, wuq4, wdh4
            spa_, spc_, hg_, tac_ = spa4, spc4, (hgate4, hgate4B), tacc4
            out_tiles, nspan = y_t, 2

        wg_r = wg_d.rearrange("(q kc p) (g i) -> g q p kc i", q=4, kc=4, p=128,
                              i=512)
        wu_r = wu_d.rearrange("(q kc p) (g i) -> g q p kc i", q=4, kc=4, p=128,
                              i=512)
        wd_r = wd_d.rearrange("(s g c m) e -> s c g m e", g=nspan, c=128, m=4)

        st = {"wg_used": [0, 0], "wu_used": 0, "wd_used": 0,
              "bk_g": [0, 0], "bk_u": [0, 0], "wd_evac": [0, 0], "spars": 0}

        def load_wg(gq, wslot, use_scr):
            marks = []
            if not use_scr:
                wait("gpsimd", "spe", st["wg_used"][wslot])
                wait("gpsimd", "swb", ffst["wg_wb"][wslot])
                for qq in range(4):
                    op("gpsimd", lambda e, gq=gq, wslot=wslot, wgb=wgb, qq=qq:
                       e.dma_start(
                           out=wgb[wslot][:, qq * 2048:(qq + 1) * 2048]
                           .rearrange("p (kc i) -> p kc i", kc=4),
                           in_=wg_r[gq, qq]), [("sw", 16)])
                    marks.append(("sw", CNT["sw"]))
                if phase == 1:
                    wait("sync", "sw", CNT["sw"])
                    op("sync", lambda e, gq=gq, wslot=wslot, wgb=wgb: e.dma_start(
                        out=wgsc[gq], in_=wgb[wslot][:, :]), [("swb", 16)])
                    ffst["wg_wb"][wslot] = CNT["swb"]
            else:
                eng, sm = ("gpsimd", "sw") if gq < 2 else ("sync", "swu")
                wait(eng, "spe", st["wg_used"][wslot])
                for qq in range(4):
                    op(eng, lambda e, gq=gq, wslot=wslot, wgb=wgb, qq=qq:
                       e.dma_start(
                           out=wgb[wslot][:, qq * 2048:(qq + 1) * 2048],
                           in_=wgsc[gq][:, qq * 2048:(qq + 1) * 2048]),
                       [(sm, 16)])
                    marks.append((sm, CNT[sm]))
            return marks

        def load_wu(gq, use_scr):
            marks = []
            if not use_scr:
                wait("gpsimd", "spe", st["wu_used"])
                wait("gpsimd", "swb", ffst["wu_wb"])
                for qq in range(4):
                    op("gpsimd", lambda e, gq=gq, wub=wub, qq=qq: e.dma_start(
                        out=wub[:, qq * 2048:(qq + 1) * 2048]
                        .rearrange("p (kc i) -> p kc i", kc=4),
                        in_=wu_r[gq, qq]), [("sw", 16)])
                    marks.append(("sw", CNT["sw"]))
                if phase == 1:
                    wait("sync", "sw", CNT["sw"])
                    op("sync", lambda e, gq=gq, wub=wub: e.dma_start(
                        out=wusc[gq], in_=wub[:, :]), [("swb", 16)])
                    ffst["wu_wb"] = CNT["swb"]
            else:
                eng, sm = ("gpsimd", "sw") if gq < 2 else ("scalar", "swd")
                wait(eng, "spe", st["wu_used"])
                wait(eng, "swb", ffst["wu_wb"])
                for qq in range(4):
                    op(eng, lambda e, gq=gq, wub=wub, qq=qq: e.dma_start(
                        out=wub[:, qq * 2048:(qq + 1) * 2048],
                        in_=wusc[gq][:, qq * 2048:(qq + 1) * 2048]), [(sm, 16)])
                    marks.append((sm, CNT[sm]))
            return marks

        def load_wd(sidx, use_scr):
            wait("gpsimd", "spe", st["wd_used"])
            wait("gpsimd", "swbd", ffst["wd_wb"])
            if not use_scr:
                op("gpsimd", lambda e, sidx=sidx, wdb=wdb, nspan=nspan: e.dma_start(
                    out=wdb[:, :].rearrange("p (g m e) -> p g m e", g=nspan, m=4),
                    in_=wd_r[sidx]), [("sw", 16)])
                mark = ("sw", CNT["sw"])
                if phase == 1:
                    wait("scalar", "sw", CNT["sw"])
                    op("scalar", lambda e, sidx=sidx, wdb=wdb: e.dma_start(
                        out=wdsc[sidx], in_=wdb[:, :]), [("swbd", 16)])
                    ffst["wd_wb"] = CNT["swbd"]
            else:
                if phase == 1:
                    src = wdsc[sidx]
                else:
                    src = wdsc[sidx // 2][:, (sidx % 2) * 16384:
                                          (sidx % 2 + 1) * 16384]
                op("gpsimd", lambda e, src=src, wdb=wdb: e.dma_start(
                    out=wdb[:, :], in_=src), [("sw", 16)])
                mark = ("sw", CNT["sw"])
            return mark

        if phase == 4:
            for e_ in ("sync", "scalar", "gpsimd"):
                wait(e_, "swb", CNT["swb"])
                wait(e_, "swbd", CNT["swbd"])

        for mg in range(NMEGA):
            if phase == 1:
                for i in range(8):
                    t = mg * 8 + i
                    if lnst["ln"]:
                        wait("sync", "sdv", lnst["ln"][-1])
                    else:
                        wait("sync", "sdv", P0_DV)
                    if mg > 0 and i == 0:
                        wait("sync", "sacc", CNT["sacc"])
                    op("sync", lambda e, t=t: e.dma_start(out=xt0, in_=x_t[t]),
                       [("sx", 16)])
                    wait("sync", "sx")
                    op("sync", lambda e, t=t: e.dma_start(out=x1_t[t], in_=xt0),
                       [("sacc", 16)])
                    ffst["acc_tile"][(1, t)] = CNT["sacc"]
                    ln_tile(xt0, lambda kc, i=i: BUF_B[:, kc * 1024 + i * 128:
                                                       kc * 1024 + (i + 1) * 128],
                            xlnsc, ready=("sx", CNT["sx"]))
                if i == 7:
                    st["bk_g"] = [max(st["bk_g"][0], CNT["sac"]),
                                  max(st["bk_g"][1], CNT["sac"])]
            use_scr = USE_SCR and ((phase != 1) or (mg > 0))
            for Qr in range(NQR):
                nhalf = 4 // nspan
                wd_lds = []
                if use_scr:
                    wd_lds.append(load_wd(Qr * nhalf, use_scr))
                for g in range(4):
                    gq = Qr * 4 + g
                    wslot = gq % 2
                    wg_q = load_wg(gq, wslot, use_scr)
                    wu_q = load_wu(gq, use_scr)
                    for p in range(2):
                        for m in range(4):
                            mp = m % 2
                            gbk = PF[4 + mp]
                            ubk = PF[2 + mp]
                            wait("tensor", "sac", st["bk_g"][mp])
                            wait("tensor", wg_q[3][0], wg_q[3][1])
                            for kc in range(16):
                                if phase == 1:
                                    rv = hq[:, kc * 1024 + p * 512:
                                            kc * 1024 + (p + 1) * 512]
                                else:
                                    rv = BUF_A[:, kc * 2048 + mg * 1024 + p * 512:
                                               kc * 2048 + mg * 1024 + (p + 1) * 512]
                                lv = wgb[wslot][:, kc * 512 + m: (kc + 1) * 512: 4]
                                op("tensor", lambda e, gbk=gbk, lv=lv, rv=rv, kc=kc:
                                   e.matmul(gbk[:, :], lv, rv, start=(kc == 0),
                                            stop=(kc == 15)),
                                   [("spe", 1)] if kc == 15 else ())
                            st["wg_used"][wslot] = CNT["spe"]
                            gmark = CNT["spe"]
                            wait("tensor", "sdv", st["bk_u"][mp])
                            wait("tensor", wu_q[3][0], wu_q[3][1])
                            for kc in range(16):
                                if phase == 1:
                                    rv = hq[:, kc * 1024 + p * 512:
                                            kc * 1024 + (p + 1) * 512]
                                else:
                                    rv = BUF_A[:, kc * 2048 + mg * 1024 + p * 512:
                                               kc * 2048 + mg * 1024 + (p + 1) * 512]
                                lv = wub[:, kc * 512 + m: (kc + 1) * 512: 4]
                                op("tensor", lambda e, ubk=ubk, lv=lv, rv=rv, kc=kc:
                                   e.matmul(ubk[:, :], lv, rv, start=(kc == 0),
                                            stop=(kc == 15)),
                                   [("spe", 1)] if kc == 15 else ())
                            st["wu_used"] = CNT["spe"]
                            umark = CNT["spe"]
                            wait("scalar", "spe", gmark)
                            wait("scalar", "sdv", st["bk_u"][mp])
                            hgb = hg_[mp]
                            op("scalar", lambda e, gbk=gbk, hgb=hgb: e.activation(
                                out=hgb, in_=gbk[:, :], func=AF.Silu), [("sac", 1)])
                            st["bk_g"][mp] = CNT["sac"]
                            hcol = hq_base + (g * 4 + m) * 1024 + p * 512
                            wait("vector", "sac", st["bk_g"][mp])
                            wait("vector", "spe", umark)
                            op("vector", lambda e, ubk=ubk, hcol=hcol, hgb=hgb, hq=hq:
                               e.tensor_mul(out=hq[:, hcol:hcol + 512], in0=hgb,
                                            in1=ubk[:, :]), [("sdv", 1)])
                            st["bk_u"][mp] = CNT["sdv"]
                        if NOSPARS:
                            continue
                        hcs = [hq_base + (g * 4 + mm) * 1024 + p * 512 for mm in range(4)]
                        wait("scalar", "sdv", CNT["sdv"])
                        for mm in range(4):
                            op("scalar", lambda e, hc=hcs[mm], mm=mm, hq=hq, spa_=spa_:
                               e.activation(out=spa_[mm], in_=hq[:, hc:hc + 512],
                                            func=AF.Abs), [("sac", 1)])
                        wait("vector", "sac")
                        prs = [(0, 1), (0, 2), (0, 3), (1, 2), (1, 3), (2, 3)]
                        for ci, (a, b) in enumerate(prs):
                            op("vector", lambda e, ci=ci, a=a, b=b, spa_=spa_, spc_=spc_:
                               e.tensor_tensor(out=spc_[ci], in0=spa_[a], in1=spa_[b],
                                               op=ALU.is_ge), [("sdv", 1)])
                        combos = [((0, 1, ALU.add), (2, ALU.add), ALU.is_ge, 1.5),
                                  ((3, 4, ALU.add), (0, ALU.subtract), ALU.is_ge, 0.5),
                                  ((1, 3, ALU.add), (5, ALU.subtract), ALU.is_le, 0.5),
                                  ((2, 4, ALU.add), (5, ALU.add), ALU.is_le, 1.5)]
                        for mm, ((ca, cb, o1), (cc, o2), kop, kth) in enumerate(combos):
                            op("vector", lambda e, mm=mm, ca=ca, cb=cb, o1=o1, spa_=spa_,
                               spc_=spc_: e.tensor_tensor(out=spa_[mm], in0=spc_[ca],
                                                          in1=spc_[cb], op=o1),
                               [("sdv", 1)])
                            op("vector", lambda e, mm=mm, cc=cc, o2=o2, spa_=spa_,
                               spc_=spc_: e.tensor_tensor(out=spa_[mm], in0=spa_[mm],
                                                          in1=spc_[cc], op=o2),
                               [("sdv", 1)])
                            op("vector", lambda e, mm=mm, kop=kop, kth=kth, spa_=spa_:
                               e.tensor_single_scalar(out=spa_[mm], in_=spa_[mm],
                                                      scalar=kth, op=kop), [("sdv", 1)])
                            op("vector", lambda e, mm=mm, hc=hcs[mm], hq=hq, spa_=spa_:
                               e.tensor_tensor(out=hq[:, hc:hc + 512],
                                               in0=hq[:, hc:hc + 512], in1=spa_[mm],
                                               op=ALU.mult), [("sdv", 1)])
                st["spars"] = CNT["sdv"]
                for half in range(nhalf):
                    if half >= len(wd_lds):
                        wd_lds.append(load_wd(Qr * nhalf + half, use_scr))
                    wd_ld = wd_lds[half]
                    wait("tensor", wd_ld[0], wd_ld[1])
                    wait("tensor", "sdv", st["spars"])
                    for tt in range(8):
                        tglob = mg * 8 + tt
                        for es in range(4):
                            par = (tt * 4 + es) % 2
                            bk = PF[par]
                            wait("tensor", "sdv", st["wd_evac"][par])
                            glist = list(range(nspan))
                            nmm = nspan * 4
                            ci = 0
                            for g2 in glist:
                                gg = half * nspan + g2
                                for m in range(4):
                                    hcol = hq_base + (gg * 4 + m) * 1024 + tt * 128
                                    wcol = (g2 * 4 + m) * 2048 + es * 512
                                    ci += 1
                                    op("tensor", lambda e, bk=bk, hcol=hcol, wcol=wcol,
                                       s0=(ci == 1), s1=(ci == nmm), hq=hq, wdb=wdb:
                                       e.matmul(bk[:, :], hq[:, hcol:hcol + 128],
                                                wdb[:, wcol:wcol + 512], start=s0,
                                                stop=s1),
                                       [("spe", 1)] if ci == nmm else ())
                            st["wd_used"] = CNT["spe"]
                            wmark = CNT["spe"]
                            wait("vector", "spe", wmark)
                            if es == 0:
                                wait("vector", "sacc",
                                     ffst["acc_busy"].get(phase, 0))
                            op("vector", lambda e, bk=bk, es=es, tac_=tac_:
                               e.tensor_scalar_mul(
                                   out=tac_[:, es * 512:(es + 1) * 512],
                                   in0=bk[:, :], scalar1=0.5), [("sdv", 1)])
                            st["wd_evac"][par] = CNT["sdv"]
                        if True:
                            wait("gpsimd", "sdv", st["wd_evac"][(tt * 4 + 3) % 2])
                            wait("gpsimd", "sacc",
                                 ffst["acc_tile"].get((phase, tglob), 0))
                            op("gpsimd", lambda e, tglob=tglob, tac_=tac_,
                               out_tiles=out_tiles: e.dma_start(
                                   out=out_tiles[tglob], in_=tac_[:, :],
                                   accum_op=ALU.add), [("sacc", 16)])
                            ffst["acc_tile"][(phase, tglob)] = CNT["sacc"]
                            ffst["acc_busy"][phase] = CNT["sacc"]

    if STAGE == "p0":
        for t in range(16):
            wait("sync", "sdv", P0_DV)
            op("sync", lambda e, t=t: e.dma_start(out=xt0, in_=x_t[t]), [("sx", 16)])
            wait("sync", "sx")
            op("sync", lambda e, t=t: e.dma_start(out=y_t[t], in_=xt0), [("sy", 16)])
        wait("sync", "sy", CNT["sy"])
        wait("tensor", "spe", P0_PE)
        return nc, Q

    ffn_pass(1)
    barrier()

    if STAGE == "p1":
        wait("sync", "sacc", CNT["sacc"])
        for t in range(16):
            wait("sync", "sy", max(0, CNT["sy"] - 16))
            op("sync", lambda e, t=t: e.dma_start(out=xt0, in_=x1_t[t]), [("sx", 16)])
            wait("sync", "sx")
            op("sync", lambda e, t=t: e.dma_start(out=y_t[t], in_=xt0), [("sy", 16)])
        wait("sync", "sy", CNT["sy"])
        return nc, Q

    for t in ([] if os.environ.get("MK_LN2", "1") == "0" else range(16)):
        if lnst["ln"]:
            wait("sync", "sdv", lnst["ln"][-1])
        op("sync", lambda e, t=t: e.dma_start(out=x1t2, in_=x1_t[t]), [("sx", 16)])
        ln_tile(x1t2, lambda kc, t=t: BUF_A[:, kc * 2048 + t * 128:
                                           kc * 2048 + (t + 1) * 128],
                xln2sc, ready=("sx", CNT["sx"]))
    LN2_DONE = CNT["sac"]

    wprep = {}

    def prep_head_weights(hh):
        sl = hh % 2
        srcs = (wq_d, wk_d, wv_d)
        qeng = ("gpsimd", "gpsimd", "gpsimd")
        qsem = ("sw", "sw", "sw")
        for wi in range(3):
            src = srcs[wi].rearrange("(h p) k -> h p k", p=128)[hh]
            stg = wstg[wi % 2]
            eng, sm = qeng[wi], qsem[wi]
            wait(eng, "spe", wprep.get(("stg", wi % 2), 0))
            wait(eng, "spe", wprep.get(("slot", sl), 0))
            op(eng, lambda e, stg=stg, src=src: e.dma_start(out=stg, in_=src),
               [(sm, 16)])
            ld = CNT[sm]
            for kc in range(16):
                wait("tensor", sm, ld)
                tpx(stg[:, kc * 128:(kc + 1) * 128],
                    wT[sl * 3 + wi][:, kc * 128:(kc + 1) * 128], "scalar")
                wprep[("stg", wi % 2)] = CNT["spe"]
        wprep[("ready", hh)] = CNT["sac"]
        msrc = mem_d.rearrange("m h d e -> h d m e")[hh]
        wait("gpsimd", "spe", wprep.get(("memfree", hh % 2), 0))
        op("gpsimd", lambda e, msrc=msrc, sl=sl: e.dma_start(
            out=memh[sl][:, :].rearrange("p (m e) -> p m e", m=4), in_=msrc),
           [("sw", 16)])
        wprep[("memld", hh)] = CNT["sw"]

    if NHEADS > 0:
        prep_head_weights(0)

    pst = {"bk01": [0, 0], "bk01s": ["sdv", "sdv"], "bk23": [0, 0],
           "bk45": [0, 0], "pb1": [0, 0], "outT": [0, 0], "exS": 0}

    for hh in range(NHEADS):
        sl = hh % 2
        wqT, wkT, wvT = wT[sl * 3 + 0], wT[sl * 3 + 1], wT[sl * 3 + 2]
        WREADY = wprep[("ready", hh)]
        MEMLD = wprep[("memld", hh)]

        wait("vector", "sexw", pst["exS"])
        op("vector", lambda e: e.memset(contrib[:, 0:136], 0.0), [("sdv", 1)])
        ZDONE = CNT["sdv"]

        def proj(wTt, out_evac):
            for p in range(4):
                pp = p % 2
                bk = PF[pp]
                wait("tensor", "sac", WREADY)
                wait("tensor", pst["bk01s"][pp], pst["bk01"][pp])
                for kc in range(16):
                    op("tensor", lambda e, bk=bk, wTt=wTt, kc=kc, p=p: e.matmul(
                        bk[:, :], wTt[:, kc * 128:(kc + 1) * 128],
                        BUF_A[:, kc * 2048 + p * 512: kc * 2048 + (p + 1) * 512],
                        start=(kc == 0), stop=(kc == 15)),
                       [("spe", 1)] if kc == 15 else ())
                out_evac(p, bk, pp)

        def k_evac(p, bk, pp):
            mk = CNT["spe"]
            wait("vector", "spe", mk)
            op("vector", lambda e, bk=bk, pp=pp: e.tensor_single_scalar(
                out=p2t[pp], in_=bk[:, :], scalar=0.0, op=ALU.min), [("sdv", 1)])
            wait("scalar", "sdv")
            op("scalar", lambda e, pp=pp: e.activation(out=p2t[pp], in_=p2t[pp],
                                                       func=AF.Exp), [("sac", 1)])
            wait("vector", "sac")
            op("vector", lambda e, bk=bk, pp=pp, p=p: e.scalar_tensor_tensor(
                out=sk_bf[:, p * 512:(p + 1) * 512], in0=bk[:, :], scalar=0.0,
                in1=p2t[pp], op0=ALU.max, op1=ALU.add), [("sdv", 1)])
            pst["bk01s"][pp], pst["bk01"][pp] = "sdv", CNT["sdv"]
        proj(wkT, k_evac)

        def v_evac(p, bk, pp):
            mk = CNT["spe"]
            wait("scalar", "spe", mk)
            op("scalar", lambda e, bk=bk, p=p: e.copy(
                out=v_bf[:, p * 512:(p + 1) * 512], in_=bk[:, :]), [("sac", 1)])
            pst["bk01s"][pp], pst["bk01"][pp] = "sac", CNT["sac"]
        proj(wvT, v_evac)
        KV_DONE = CNT["sac"]

        SKV = CNT["sdv"]
        for c in range(NCH):
            par = c % 2
            wait("tensor", "sdv", SKV)
            wait("tensor", "sac", KV_DONE)
            tpx(sk_bf[:, c * 128:(c + 1) * 128], kpT[:, c * 128:(c + 1) * 128], "vector")
            tpx(v_bf[:, c * 128:(c + 1) * 128], vT[:, c * 128:(c + 1) * 128], "vector")
            pst["pb1"][par] = CNT["sdv"]
            wait("vector", "sdv", ZDONE if c == 0 else CNT["sdv"])
            op("vector", lambda e, c=c: e.tensor_copy(
                out=spv[:, c * 132: c * 132 + 129], in_=contrib[:, 0:129]), [("sdv", 1)])
            SNAP = CNT["sdv"]
            bkv = PF[par]
            wait("tensor", pst["bk01s"][par], pst["bk01"][par])
            wait("tensor", "sdv", pst["pb1"][par])
            op("tensor", lambda e, bkv=bkv, c=c: e.matmul(
                bkv[:, 0:128], kpT[:, c * 128:(c + 1) * 128],
                vT[:, c * 128:(c + 1) * 128], start=True, stop=True), [("spe", 1)])
            op("tensor", lambda e, bkv=bkv, c=c: e.matmul(
                bkv[:, 128:129], kpT[:, c * 128:(c + 1) * 128], onescol,
                start=True, stop=True), [("spe", 1)])
            KVM = CNT["spe"]
            wait("vector", "spe", KVM)
            wait("vector", "sdv", SNAP)
            op("vector", lambda e, bkv=bkv: e.tensor_add(out=contrib[:, 0:129],
                                                         in0=contrib[:, 0:129],
                                                         in1=bkv[:, 0:129]), [("sdv", 1)])
            pst["bk01s"][par], pst["bk01"][par] = "sdv", CNT["sdv"]
        SCAN_DONE = CNT["sdv"]

        wait("gpsimd", "sdv", SCAN_DONE)

        def ex_write_S(e, hh=hh):
            with e.If_eq(regs["pvr"], 0):
                e.dma_start(out=exd[hh, :, 0:130],
                            in_=contrib[:, 0:130]).then_inc(SEM["sexw"], 16)
            with e.Else():
                e.dma_start(out=exd[H + hh, :, 0:130],
                            in_=contrib[:, 0:130]).then_inc(SEM["sexw"], 16)
        if USE_EX:
            em("gpsimd", ex_write_S)
            CNT["sexw"] += 16
        pst["exS"] = CNT["sexw"]

        def q_evac(p, bk, pp):
            mk = CNT["spe"]
            wait("scalar", "spe", mk)
            op("scalar", lambda e, bk=bk, pp=pp, p=p: e.activation(
                out=p2t[pp], in_=bk[:, :], func=AF.Identity,
                accum_out=qsumt[:, p:p + 1]), [("sac", 1)])
            QS = CNT["sac"]
            wait("vector", "sac", QS)
            op("vector", lambda e, bk=bk, pp=pp: e.tensor_single_scalar(
                out=p2t[pp], in_=bk[:, :], scalar=0.0, op=ALU.min), [("sdv", 1)])
            wait("scalar", "sdv")
            op("scalar", lambda e, pp=pp: e.activation(out=p2t[pp], in_=p2t[pp],
                                                       func=AF.Exp), [("sac", 1)])
            wait("vector", "sac")
            op("vector", lambda e, bk=bk, pp=pp, p=p: e.scalar_tensor_tensor(
                out=sq_bf[:, p * 512:(p + 1) * 512], in0=bk[:, :], scalar=0.0,
                in1=p2t[pp], op0=ALU.max, op1=ALU.add), [("sdv", 1)])
            pst["bk01s"][pp], pst["bk01"][pp] = "sdv", CNT["sdv"]
        proj(wqT, q_evac)
        SQ_DONE = CNT["sdv"]

        wait("vector", "sdv", SQ_DONE)
        op("vector", lambda e: e.reduce_sum(out=stat1, in_=qsumt, axis=AX.X), [("sdv", 1)])
        wait("vector", "sdv")
        op("vector", lambda e: e.tensor_copy(out=qsum_bf, in_=stat1), [("sdv", 1)])
        QSB = CNT["sdv"]
        wait("tensor", "sdv", QSB)
        wait("tensor", "sdv", pst["bk23"][0])
        op("tensor", lambda e, hh=hh: e.matmul(PF[2][0:1, 0:4], qsum_bf,
                                               mnT[:, hh:64:16], start=True, stop=True),
           [("spe", 1)])
        wait("vector", "spe")
        op("vector", lambda e: e.tensor_copy(out=contrib[0:1, 130:134],
                                             in_=PF[2][0:1, 0:4]), [("sdv", 1)])
        RELC = CNT["sdv"]

        wait("gpsimd", "sdv", RELC)
        wait("gpsimd", "sexw", CNT["sexw"])

        def ex_write_rel(e, hh=hh):
            with e.If_eq(regs["pvr"], 0):
                e.dma_start(out=exd[hh, 0:1, 130:134],
                            in_=contrib[0:1, 130:134]).then_inc(SEM["sexw"], 16)
            with e.Else():
                e.dma_start(out=exd[H + hh, 0:1, 130:134],
                            in_=contrib[0:1, 130:134]).then_inc(SEM["sexw"], 16)
        if USE_EX:
            em("gpsimd", ex_write_rel)
            CNT["sexw"] += 16
        wait("gpsimd", "sexw", CNT["sexw"])

        def ex_write_flag(e, hh=hh):
            with e.If_eq(regs["pvr"], 0):
                e.dma_start(out=exf[0:1, hh:hh + 1],
                            in_=flagt[0:1, hh:hh + 1]).then_inc(SEM["sexw"], 16)
            with e.Else():
                e.dma_start(out=exf[1:2, hh:hh + 1],
                            in_=flagt[0:1, hh:hh + 1]).then_inc(SEM["sexw"], 16)
        if USE_EX:
            em("gpsimd", ex_write_flag)
            CNT["sexw"] += 16

        wait("tensor", "sdv", RELC)
        for c in range(NCH):
            par = c % 2
            bsc = PF[2 + par]
            bnm = PF[4 + par]
            wait("tensor", "sdv", pst["bk23"][par])
            op("tensor", lambda e, bsc=bsc, c=c: e.matmul(
                bsc[:, 0:128], sk_bf[:, c * 128:(c + 1) * 128],
                sq_bf[:, c * 128:(c + 1) * 128], start=True, stop=True), [("spe", 1)])
            SCM = CNT["spe"]
            wait("vector", "spe", SCM)
            op("vector", lambda e, bsc=bsc, par=par: e.tensor_mul(
                out=scT[par], in0=bsc[:, 0:128], in1=trimask), [("sdv", 1)])
            pst["bk23"][par] = CNT["sdv"]
            SCS = CNT["sdv"]
            wait("tensor", "sdv", pst["bk45"][par])
            wait("tensor", "sdv", SCS)
            op("tensor", lambda e, bnm=bnm, par=par, c=c: e.matmul(
                bnm[:, 0:128], scT[par], vT[:, c * 128:(c + 1) * 128],
                start=True, stop=False), [("spe", 1)])
            op("tensor", lambda e, bnm=bnm, c=c: e.matmul(
                bnm[:, 0:128], sq_bf[:, c * 128:(c + 1) * 128],
                spv[:, c * 132: c * 132 + 128], start=False, stop=True), [("spe", 1)])
            op("tensor", lambda e, bnm=bnm, par=par: e.matmul(
                bnm[:, 128:129], scT[par], onescol, start=True, stop=False), [("spe", 1)])
            op("tensor", lambda e, bnm=bnm, c=c: e.matmul(
                bnm[:, 128:129], sq_bf[:, c * 128:(c + 1) * 128],
                spv[:, c * 132 + 128: c * 132 + 129], start=False, stop=True),
               [("spe", 1)])
            NMM = CNT["spe"]
            wait("vector", "spe", NMM)
            op("vector", lambda e, bnm=bnm, c=c: e.tensor_copy(
                out=numb[:, c * 128:(c + 1) * 128], in_=bnm[:, 0:128]), [("sdv", 1)])
            op("vector", lambda e, bnm=bnm, c=c: e.tensor_copy(
                out=denA[:, c:c + 1], in_=bnm[:, 128:129]), [("sdv", 1)])
            pst["bk45"][par] = CNT["sdv"]

        def ex_poll_read(e, hh=hh):
            fl = regs["fl"]

            def poll_body(src_flag, src_data):
                e.reg_mov(fl, 1)
                with e.While(fl):
                    e.dma_start(out=flagin[0:1, 0:1],
                                in_=src_flag).then_inc(SEM["spoll"], 16)
                    e.nop(cycle_cnt=2000)
                    e.tensor_tensor(out=flagin[0:1, 1:2], in0=flagin[0:1, 0:1],
                                    in1=flagt[0:1, hh:hh + 1], op=ALU.subtract)
                    e.nop(cycle_cnt=400)
                    e.reg_load(fl, flagin[0:1, 1:2].bitcast(I32))
                e.dma_start(out=exin[:, 0:136], in_=src_data).then_inc(
                    SEM["sexr"], 16)

            with e.If_eq(regs["pvr"], 0):
                poll_body(exf[1:2, hh:hh + 1], exd[H + hh, :, :])
            with e.Else():
                poll_body(exf[0:1, hh:hh + 1], exd[hh, :, :])
        if USE_EX:
            em("gpsimd", ex_poll_read)
            CNT["sexr"] += 16
        else:
            op("gpsimd", lambda e: e.dma_start(out=exin[:, 0:136],
                                               in_=exd[hh, :, :]), [("sexr", 16)])
        EXR = CNT["sexr"]

        wait("vector", "sexr", EXR)
        op("vector", lambda e: e.tensor_scalar_mul(out=srem[:, 0:129], in0=exin[:, 0:129],
                                                   scalar1=usest[:, 0:1]), [("sdv", 1)])
        SREMD = CNT["sdv"]
        op("vector", lambda e: e.tensor_add(out=reltot, in0=contrib[0:1, 130:134],
                                            in1=exin[0:1, 130:134]), [("sdv", 1)])
        wait("vector", "sdv")
        op("vector", lambda e: e.scalar_tensor_tensor(out=relsc[0:1, 0:4], in0=reltot,
                                                      scalar=SCALE / 4096.0, in1=bneg,
                                                      op0=ALU.mult, op1=ALU.add),
           [("sdv", 1)])
        wait("vector", "sdv")
        op("vector", lambda e: e.reduce_max(out=relsc[0:1, 4:5], in_=relsc[0:1, 0:4],
                                            axis=AX.X), [("sdv", 1)])
        wait("vector", "sdv")
        op("vector", lambda e: e.tensor_scalar_mul(out=relsc[0:1, 5:6],
                                                   in0=relsc[0:1, 4:5], scalar1=-1.0),
           [("sdv", 1)])
        wait("scalar", "sdv")
        op("scalar", lambda e: e.activation(out=wfin[0:1, 0:4], in_=relsc[0:1, 0:4],
                                            func=AF.Exp, bias=relsc[0:1, 5:6],
                                            accum_out=relsc[0:1, 6:7]), [("sac", 1)])
        wait("vector", "sac")
        op("vector", lambda e: e.reciprocal(out=relsc[0:1, 7:8], in_=relsc[0:1, 6:7]),
           [("sdv", 1)])
        wait("vector", "sdv")
        op("vector", lambda e: e.tensor_scalar_mul(out=wfin[0:1, 0:4], in0=wfin[0:1, 0:4],
                                                   scalar1=relsc[0:1, 7:8]), [("sdv", 1)])
        wait("vector", "sdv")
        op("vector", lambda e: e.tensor_mul(out=wfin[0:1, 0:4], in0=wfin[0:1, 0:4],
                                            in1=em1), [("sdv", 1)])
        wait("vector", "sdv")
        op("vector", lambda e, hh=hh: e.tensor_scalar_mul(
            out=wfin[0:1, 0:4], in0=wfin[0:1, 0:4], scalar1=sigg[0:1, hh:hh + 1]),
           [("sdv", 1)])
        wait("vector", "sdv")
        op("vector", lambda e, hh=hh: e.tensor_copy(out=wfin[0:1, 4:5],
                                                    in_=omg[0:1, hh:hh + 1]), [("sdv", 1)])
        wait("vector", "sdv")
        op("vector", lambda e: e.tensor_copy(out=wfin_bf[0:1, 0:5], in_=wfin[0:1, 0:5]),
           [("sdv", 1)])
        WFB = CNT["sdv"]
        wait("tensor", "sdv", WFB)
        wait("tensor", "sdv", pst["bk23"][0])
        op("tensor", lambda e: e.matmul(PF[2][:, 384:389], onesrow, wfin_bf[0:1, 0:5],
                                        start=True, stop=True), [("spe", 1)])
        wait("vector", "spe")
        op("vector", lambda e: e.tensor_copy(out=wrep[:, 0:5], in_=PF[2][:, 384:389]),
           [("sdv", 1)])
        WREP = CNT["sdv"]
        pst["bk23"][0] = WREP

        for c in range(NCH):
            par = c % 2
            brm = PF[2 + par]
            wait("tensor", "sdv", pst["bk23"][par])
            wait("tensor", "sdv", SREMD)
            op("tensor", lambda e, brm=brm, c=c: e.matmul(
                brm[:, 0:128], sq_bf[:, c * 128:(c + 1) * 128], srem[:, 0:128],
                start=True, stop=True), [("spe", 1)])
            op("tensor", lambda e, brm=brm, c=c: e.matmul(
                brm[:, 128:129], sq_bf[:, c * 128:(c + 1) * 128], srem[:, 128:129],
                start=True, stop=True), [("spe", 1)])
            RMM = CNT["spe"]
            wait("vector", "spe", RMM)
            op("vector", lambda e, brm=brm, c=c: e.tensor_add(
                out=numb[:, c * 128:(c + 1) * 128], in0=numb[:, c * 128:(c + 1) * 128],
                in1=brm[:, 0:128]), [("sdv", 1)])
            op("vector", lambda e, brm=brm, c=c: e.tensor_copy(
                out=denR[:, c:c + 1], in_=brm[:, 128:129]), [("sdv", 1)])
            pst["bk23"][par] = CNT["sdv"]
        B1_DONE = CNT["sdv"]

        wait("vector", "sdv", B1_DONE)
        op("vector", lambda e: e.tensor_add(out=rdent, in0=denA, in1=denR), [("sdv", 1)])
        wait("vector", "sdv")
        op("vector", lambda e: e.tensor_single_scalar(out=rdent, in_=rdent,
                                                      scalar=EPS_AT, op=ALU.max),
           [("sdv", 1)])
        wait("vector", "sdv")
        op("vector", lambda e: e.reciprocal(out=rdent, in_=rdent), [("sdv", 1)])
        wait("vector", "sdv")
        op("vector", lambda e: e.tensor_scalar_mul(out=rdent, in0=rdent,
                                                   scalar1=wrep[:, 4:5]), [("sdv", 1)])
        DEN_DONE = CNT["sdv"]

        wait("tensor", "sdv", pst["bk23"][0])
        wait("tensor", "sdv", pst["bk23"][1])
        for c in range(NCH):
            op("tensor", lambda e, c=c, hh=hh: e.matmul(
                PF[2][:, c * 4:(c + 1) * 4], sq_bf[:, c * 128:(c + 1) * 128],
                mnT[:, hh:64:16], start=True, stop=True),
               [("spe", 1)] if c == NCH - 1 else ())
        NRM = CNT["spe"]
        wait("vector", "spe", NRM)
        op("vector", lambda e: e.tensor_single_scalar(out=sfacall, in_=PF[2][:, 0:64],
                                                      scalar=EPS_AT, op=ALU.max),
           [("sdv", 1)])
        pst["bk23"][0] = CNT["sdv"]
        pst["bk23"][1] = CNT["sdv"]
        wait("vector", "sdv")
        op("vector", lambda e: e.reciprocal(out=sfacall, in_=sfacall), [("sdv", 1)])
        wait("vector", "sdv")
        wait("vector", "sdv", WREP)
        for m in range(4):
            op("vector", lambda e, m=m: e.tensor_scalar_mul(
                out=sfacall[:, m:64:4], in0=sfacall[:, m:64:4],
                scalar1=wrep[:, m:m + 1]), [("sdv", 1)])
        SFAC_DONE = CNT["sdv"]

        for c in range(NCH):
            par = c % 2
            bmm = PF[4 + par]
            wait("tensor", "sdv", pst["bk45"][par])
            wait("tensor", "sw", MEMLD)
            op("tensor", lambda e, bmm=bmm, c=c, sl=sl: e.matmul(
                bmm[:, 0:512], sq_bf[:, c * 128:(c + 1) * 128], memh[sl][:, :],
                start=True, stop=True), [("spe", 1)])
            MMM = CNT["spe"]
            wait("vector", "spe", MMM)
            wait("vector", "sdv", SFAC_DONE)
            wait("vector", "sdv", DEN_DONE)
            if NOLOC:
                op("vector", lambda e: e.memset(p2out, 0.0), [("sdv", 1)])
            else:
                op("vector", lambda e, c=c: e.tensor_scalar_mul(
                    out=p2out, in0=numb[:, c * 128:(c + 1) * 128],
                    scalar1=rdent[:, c:c + 1]), [("sdv", 1)])
            wait("vector", "sdv")
            for m in ([] if NOMEM else range(4)):
                op("vector", lambda e, bmm=bmm, m=m, c=c: e.scalar_tensor_tensor(
                    out=p2out, in0=bmm[:, m * 128:(m + 1) * 128],
                    scalar=sfacall[:, c * 4 + m: c * 4 + m + 1], in1=p2out,
                    op0=ALU.mult, op1=ALU.add), [("sdv", 1)])
                wait("vector", "sdv")
            pst["bk45"][par] = CNT["sdv"]
            op("vector", lambda e, par=par: e.tensor_copy(out=scT[par], in_=p2out),
               [("sdv", 1)])
            CST = CNT["sdv"]
            wait("tensor", "sdv", CST)
            mk = tpx(scT[par], BUF_B[:, hh * 2048 + c * 128: hh * 2048 + (c + 1) * 128],
                     "scalar")
            pst["outT"][par] = mk

        if hh + 1 < NHEADS:
            wprep[("slot", (hh + 1) % 2)] = CNT["spe"]
            wprep[("memfree", (hh + 1) % 2)] = CNT["spe"]
            prep_head_weights(hh + 1)

    if STAGE == "dbg2":
        def dump2(src_ap, t):
            wait("vector", "sdv", CNT["sdv"])
            wait("vector", "sac", CNT["sac"])
            wait("vector", "sy", CNT["sy"])
            op("vector", lambda e, src_ap=src_ap: e.tensor_copy(out=x2t, in_=src_ap),
               [("sdv", 1)])
            wait("sync", "sdv", CNT["sdv"])
            op("sync", lambda e, t=t: e.dma_start(out=y_t[t], in_=x2t), [("sy", 16)])
        for t in range(16):
            dump2(BUF_B[:, t * 2048:(t + 1) * 2048], t)
        wait("sync", "sy", CNT["sy"])
        return nc, Q

    if STAGE == "dbg":
        def dump_bf(src_ap, t):
            wait("vector", "sdv", CNT["sdv"])
            wait("vector", "sac", CNT["sac"])
            wait("vector", "sy", CNT["sy"])
            op("vector", lambda e, src_ap=src_ap: e.tensor_copy(out=x2t, in_=src_ap),
               [("sdv", 1)])
            wait("sync", "sdv", CNT["sdv"])
            op("sync", lambda e, t=t: e.dma_start(out=y_t[t], in_=x2t), [("sy", 16)])
            wait("sync", "sy", CNT["sy"])

        dump_bf(numb[:, :], 0)
        wait("vector", "sy", CNT["sy"])
        op("vector", lambda e: e.memset(x2t, 0.0), [("sdv", 1)])
        wait("vector", "sdv")
        op("vector", lambda e: e.tensor_copy(out=x2t[:, 0:16], in_=denA), [("sdv", 1)])
        op("vector", lambda e: e.tensor_copy(out=x2t[:, 16:32], in_=denR), [("sdv", 1)])
        op("vector", lambda e: e.tensor_copy(out=x2t[:, 32:48], in_=rdent), [("sdv", 1)])
        op("vector", lambda e: e.tensor_copy(out=x2t[:, 48:180], in_=srem[:, 0:132]),
           [("sdv", 1)])
        op("vector", lambda e: e.tensor_copy(out=x2t[:, 180:316], in_=contrib),
           [("sdv", 1)])
        wait("sync", "sdv", CNT["sdv"])
        op("sync", lambda e: e.dma_start(out=y_t[1], in_=x2t), [("sy", 16)])
        dump_bf(sq_bf[:, :], 2)
        dump_bf(sk_bf[:, :], 3)
        dump_bf(v_bf[:, :], 4)
        dump_bf(spv[:, 0:2048], 5)
        dump_bf(kpT[:, :], 6)
        dump_bf(vT[:, :], 7)
        DH = int(os.environ.get("MK_DH", "0"))
        dump_bf(BUF_B[:, DH * 2048:(DH + 1) * 2048], 8)
        dump_bf(BUF_A[:, 0:2048], 9)
        wait("sync", "sy", CNT["sy"])
        return nc, Q

    barrier()

    if STAGE == "p2":
        wait("sync", "sacc", CNT["sacc"])
        for t in range(16):
            op("sync", lambda e, t=t: e.dma_start(out=x2t, in_=x1_t[t]), [("sx", 16)])
            wait("sync", "sx")
            op("sync", lambda e, t=t: e.dma_start(out=y_t[t], in_=x2t), [("sy", 16)])
        wait("sync", "sy", CNT["sy"])
        return nc, Q

    wo_r = wo_d.rearrange("(q ec p) (h dd) -> h q p ec dd", q=4, ec=4, p=128,
                          dd=128)
    wst3 = {"stg": 0, "tp": [0, 0]}
    for h2 in range(H):
        wait("gpsimd", "spe", wst3["stg"])
        wo_q = []
        for qq in range(4):
            op("gpsimd", lambda e, h2=h2, qq=qq: e.dma_start(
                out=wostg[:, qq * 512:(qq + 1) * 512]
                .rearrange("p (ec dd) -> p ec dd", ec=4), in_=wo_r[h2, qq]),
               [("sw", 16)])
            wo_q.append(CNT["sw"])
        for ec in range(16):
            wait("tensor", "sw", wo_q[ec // 4])
            tpx(wostg[:, ec * 128:(ec + 1) * 128],
                woT[:, h2 * 2048 + ec * 128: h2 * 2048 + (ec + 1) * 128], "scalar")
            wst3["stg"] = CNT["spe"]
    WOT_DONE = CNT["sac"]

    if STAGE == "dbgwo":
        def dump3(src_ap, t):
            wait("vector", "sdv", CNT["sdv"])
            wait("vector", "sac", CNT["sac"])
            wait("vector", "sy", CNT["sy"])
            op("vector", lambda e, src_ap=src_ap: e.tensor_copy(out=x2t, in_=src_ap),
               [("sdv", 1)])
            wait("sync", "sdv", CNT["sdv"])
            op("sync", lambda e, t=t: e.dma_start(out=y_t[t], in_=x2t), [("sy", 16)])
        for t in range(16):
            dump3(woT[:, t * 2048:(t + 1) * 2048], t)
        wait("sync", "sy", CNT["sy"])
        return nc, Q

    p3 = {"evac": [0, 0], "x2free": 0}
    for t in range(16):
        if lnst["ln"]:
            wait("sync", "sdv", lnst["ln"][-1])
        wait("sync", "sy", p3["x2free"])
        op("sync", lambda e, t=t: e.dma_start(out=x2t, in_=x1_t[t]), [("sx", 16)])
        XLD = CNT["sx"]
        for es in range(4):
            par = es % 2
            bk = PF[par]
            wait("tensor", "sdv", p3["evac"][par])
            wait("tensor", "sac", WOT_DONE)
            for h2 in range(16):
                op("tensor", lambda e, bk=bk, h2=h2, t=t, es=es: e.matmul(
                    bk[:, :], BUF_B[:, h2 * 2048 + t * 128: h2 * 2048 + (t + 1) * 128],
                    woT[:, h2 * 2048 + es * 512: h2 * 2048 + (es + 1) * 512],
                    start=(h2 == 0), stop=(h2 == 15)),
                   [("spe", 1)] if h2 == 15 else ())
            PM = CNT["spe"]
            wait("vector", "spe", PM)
            wait("vector", "sx", XLD)
            op("vector", lambda e, bk=bk, es=es: e.tensor_add(
                out=x2t[:, es * 512:(es + 1) * 512],
                in0=x2t[:, es * 512:(es + 1) * 512], in1=bk[:, :]), [("sdv", 1)])
            p3["evac"][par] = CNT["sdv"]
        X2DONE = CNT["sdv"]
        wait("sync", "sdv", X2DONE)
        op("sync", lambda e, t=t: e.dma_start(out=y_t[t], in_=x2t), [("sy", 16)])
        p3["x2free"] = CNT["sy"]
        ln_tile(x2t, lambda kc, t=t: BUF_A[:, kc * 2048 + t * 128:
                                          kc * 2048 + (t + 1) * 128],
                xln3sc, ready=("sdv", X2DONE))

    barrier()
    wait("gpsimd", "sy", CNT["sy"])

    if STAGE == "p3":
        wait("sync", "sy", CNT["sy"])
        return nc, Q

    ffn_pass(4)

    wait("sync", "sacc", CNT["sacc"])
    wait("sync", "sy", CNT["sy"])
    return nc, Q


def _assemble(nc, Q):
    with nc.Block() as block:
        block.sync(lambda e: [f(e) for f in Q["sync"]])
        block.tensor(lambda e: [f(e) for f in Q["tensor"]])
        block.vector(lambda e: [f(e) for f in Q["vector"]])
        block.scalar(lambda e: [f(e) for f in Q["scalar"]])
        block.gpsimd(lambda e: [f(e) for f in Q["gpsimd"]])
    return nc


_CACHE = {}


def _get_nc():
    if STAGE not in _CACHE:
        nc, Q = build()
        _CACHE[STAGE] = _assemble(nc, Q)
    return _CACHE[STAGE]


def kernel(L0_attn_hidden_states, L0_attn_memories, L0_attn_memory_norms,
           L0_attn_w_q, L0_attn_w_k, L0_attn_w_v, L0_attn_w_o, L0_attn_gate,
           L0_ffn_w_gate, L0_ffn_w_up, L0_ffn_w_down):
    import ml_dtypes
    nc = _get_nc()
    f32 = np.float32
    x = np.ascontiguousarray(np.asarray(L0_attn_hidden_states, dtype=f32))
    args = {
        "wq": np.ascontiguousarray(np.asarray(L0_attn_w_q, dtype=f32)),
        "wk": np.ascontiguousarray(np.asarray(L0_attn_w_k, dtype=f32)),
        "wv": np.ascontiguousarray(np.asarray(L0_attn_w_v, dtype=f32)),
        "wo": np.ascontiguousarray(np.asarray(L0_attn_w_o, dtype=f32)),
        "wg_in": np.ascontiguousarray(np.asarray(L0_ffn_w_gate, dtype=f32)),
        "wu_in": np.ascontiguousarray(np.asarray(L0_ffn_w_up, dtype=f32)),
        "wd_in": np.ascontiguousarray(np.asarray(L0_ffn_w_down, dtype=f32)),
        "mem_in": np.ascontiguousarray(np.asarray(L0_attn_memories, dtype=f32)),
        "mn_in": np.ascontiguousarray(np.asarray(L0_attn_memory_norms, dtype=f32)),
        "gate_in": np.ascontiguousarray(
            np.asarray(L0_attn_gate, dtype=f32)).reshape(1, H),
        "trimask_in": np.triu(np.ones((128, 128))).astype(ml_dtypes.bfloat16),
        "identb_in": np.eye(128).astype(ml_dtypes.bfloat16),
        "onescol_in": np.ones((128, 1), dtype=ml_dtypes.bfloat16),
        "onesrow_in": np.ones((1, 128), dtype=ml_dtypes.bfloat16),
    }
    nonce = int(np.random.randint(1, 2 ** 20))
    flagv = np.zeros((1, 24), dtype=np.float32)
    flagv[0, :16] = (nonce + 1 + np.arange(16)).astype(np.float32)
    in_maps = []
    for c in range(8):
        b, half = c // 2, c % 2
        m = dict(args)
        m["x_in"] = np.ascontiguousarray(x[b, half * TOK:(half + 1) * TOK, :])
        m["pidx_in"] = np.array([[half, 1 - half]], dtype=np.int32)
        m["usest_in"] = np.full((128, 1), float(half), dtype=f32)
        m["flagv_in"] = flagv
        in_maps.append(m)
    res = run_bass_kernel_spmd(nc, in_maps, core_ids=list(range(8)))
    out = np.empty((4, 4096, D), dtype=f32)
    for c in range(8):
        b, half = c // 2, c % 2
        out[b, half * TOK:(half + 1) * TOK, :] = res.results[c]["y"]
    return out
